# version 15
# baseline (speedup 1.0000x reference)
import numpy as np
import ml_dtypes
from contextlib import ExitStack
import concourse.bass as bass
import concourse.mybir as mybir
from concourse.bass_utils import run_bass_kernel_spmd

F32 = mybir.dt.float32
BF16 = mybir.dt.bfloat16
I32 = mybir.dt.int32
I8 = mybir.dt.int8
AF = mybir.ActivationFunctionType
ALU = mybir.AluOpType
AX = mybir.AxisListType

NCORES = 8
S, D, HF = 2048, 1024, 4096
NSEQ = 2
T = NSEQ * S
DC, HC = D // 128, HF // 128
TB = 256
TQ = 256
TW = 256
GA = 512
GB = 1024
EPS_LN = 1e-5
EPS_NORM = 1e-12
EPS_Q = 1e-5


def build():
    nc = bass.Bass("TRN2", target_bir_lowering=False, detect_race_conditions=False)

    d_xT = nc.dram_tensor("xT", [NSEQ, D, S], F32, kind="ExternalInput")
    d_xtok = nc.dram_tensor("xtok", [NSEQ, S, D], F32, kind="ExternalInput")
    d_w = {n: nc.dram_tensor(n, [D, D], F32, kind="ExternalInput")
           for n in ["wqT", "wkT", "wvT", "w1T"]}
    d_w2T = nc.dram_tensor("w2T", [D, HF], F32, kind="ExternalInput")
    d_w3T = nc.dram_tensor("w3T", [HF, D], F32, kind="ExternalInput")
    d_wgT = nc.dram_tensor("wgT", [D, HF], F32, kind="ExternalInput")
    d_wuT = nc.dram_tensor("wuT", [D, HF], F32, kind="ExternalInput")
    d_wdT = nc.dram_tensor("wdT", [HF, D], F32, kind="ExternalInput")
    d_bq = nc.dram_tensor("bq", [128, DC], F32, kind="ExternalInput")
    d_bk = nc.dram_tensor("bk", [128, DC], F32, kind="ExternalInput")
    d_b1 = nc.dram_tensor("b1", [128, DC], F32, kind="ExternalInput")
    d_b2 = nc.dram_tensor("b2", [128, HC], F32, kind="ExternalInput")
    d_bv = nc.dram_tensor("bvrow", [1, D], F32, kind="ExternalInput")
    d_b3 = nc.dram_tensor("b3row", [1, D], F32, kind="ExternalInput")
    d_id16 = nc.dram_tensor("ident16", [128, 128], BF16, kind="ExternalInput")
    d_y = nc.dram_tensor("y", [T, D], F32, kind="ExternalOutput")

    d_outf = nc.dram_tensor("scr_outf", [D, T], F32)
    d_h1h = nc.dram_tensor("scr_h1h", [D, T], BF16)
    d_h1l = nc.dram_tensor("scr_h1l", [D, T], BF16)
    d_x1 = nc.dram_tensor("scr_x1", [T, D], F32)
    d_xq = nc.dram_tensor("scr_xq", [T, D], BF16)
    d_pq = nc.dram_tensor("scr_pq", [T, HF], BF16)

    RA = nc.alloc_sbuf_tensor("RA", [128, 32768], BF16)
    RB = nc.alloc_sbuf_tensor("RB", [128, 32768], BF16)
    RC = nc.alloc_sbuf_tensor("RC", [128, 16384], BF16)
    RK = nc.alloc_sbuf_tensor("RK", [128, 22528], BF16)
    RA32, RB32, RK32 = RA.bitcast(F32), RB.bitcast(F32), RK.bitcast(F32)
    RKI32 = RK.bitcast(I32)
    RCI8 = RC.bitcast(I8)
    col_pp = None

    KnT_hi = RA[:, 0:16384].rearrange("p (c t) -> p c t", c=DC)
    KnT_lo = RA[:, 16384:32768].rearrange("p (c t) -> p c t", c=DC)
    V_hi = RB[:, 0:16384].rearrange("p (c d) -> p c d", c=16)
    V_lo = RB[:, 16384:32768].rearrange("p (c d) -> p c d", c=16)
    Wc_hi = RC[:, 0:8192].rearrange("p (c d) -> p c d", c=DC)
    Wc_lo = RC[:, 8192:16384].rearrange("p (c d) -> p c d", c=DC)
    def w23_views(par):
        base = par * 16384
        mk2 = lambda off: RA[:, base + off: base + off + 4096].rearrange(
            "p (c d) -> p c d", c=DC)
        mk3 = lambda off: RA[:, base + off: base + off + 4096].rearrange(
            "p (c d) -> p c d", c=4)
        return mk2(0), mk2(4096), mk3(8192), mk3(12288)
    RC32 = RC.bitcast(F32)
    wrc_st = RC32[:, 0:4096].rearrange("p (b t) -> p b t", b=2)
    wrc_gt = RC32[:, 4096:6144]
    wrc_lt = RC32[:, 6144:8192]
    a_acc = RB32.rearrange("p (c d) -> p c d", c=16)
    wgq = RA.rearrange("p (c h) -> p c h", c=DC)
    wuq = RB.rearrange("p (c h) -> p c h", c=DC)
    wdq8 = RCI8.rearrange("p (c d) -> p c d", c=HC)
    wdq = RA.rearrange("p (c d) -> p c d", c=HC)
    pqT_g = RB.rearrange("p (c t) -> p c t", c=HC)

    def rk(off, shape, dt=F32):
        n = int(np.prod(shape[1:]))
        sz = 2 if dt == BF16 else 4
        assert off % 4 == 0 and off + n * sz <= 45056, (off, n, sz)
        if dt == F32:
            base = RK32[:, off // 4: off // 4 + n]
        elif dt == BF16:
            base = RK[:, off // 2: off // 2 + n]
        elif dt == I32:
            base = RKI32[:, off // 4: off // 4 + n]
        if len(shape) == 3:
            return base.rearrange("p (a b) -> p a b", a=shape[1])
        if len(shape) == 4:
            return base.rearrange("p (a b c) -> p a b c", a=shape[1], b=shape[2])
        return base

    xt_f = rk(0, [128, DC, 256], F32)
    xt_hi = rk(8192, [128, DC, 256], BF16)
    xt_lo = rk(12288, [128, DC, 256], BF16)
    qf = rk(16384, [128, DC, 256], F32)
    vb_scr = rk(16384, [128, 512], F32)
    qsq = rk(24576, [128, DC, 256], F32)
    qn_hi = rk(32768, [128, DC, 256], BF16)
    qn_lo = rk(36864, [128, DC, 256], BF16)
    e_f = rk(16384, [128, 256], F32)
    e_hi = rk(24576, [128, 16, 256], BF16)
    e_lo = rk(0, [128, 16, 256], BF16)
    outf_ev = rk(16384, [128, DC, 256], F32)
    w1_h1f = rk(24576, [128, 2, 256], F32)
    bvbc = rk(32768, [128, D], F32)
    rowd = rk(36864, [1, D], F32)
    w23_h1h = rk(0, [128, 2, DC, TW], BF16)
    w23_h1l = rk(8192, [128, 2, DC, TW], BF16)
    w23_h2f = rk(16384, [128, 2, TW], F32)
    w23_h2h = rk(18432, [128, 2, TW], BF16)
    w23_h2l = rk(19456, [128, 2, TW], BF16)
    w23_wst = rk(20480, [128, DC, 256], F32)
    mean_st = rk(20480, [128, 4096], F32)
    wd_st = rk(28672, [128, 2, 1024], F32)
    wd_lt = rk(36864, [128, 1024], F32)
    b3bc = rk(40960, [128, D], F32)
    rowd_w = rk(36864, [1, D], F32)
    ln_x = rk(0, [128, 2, D], F32)
    ln_z = rk(8192, [128, D], F32)
    ln_zsq = rk(12288, [128, D], F32)
    ln_x1 = rk(16384, [128, 2, D], F32)
    ln_c32 = rk(24576, [128, D], I32)
    ln_cf = rk(28672, [128, D], F32)
    ln_d = rk(32768, [128, D], F32)
    ln_gt = rk(36864, [128, D], F32)
    ln_xq = rk(40960, [128, 2, D], BF16)
    wr_st = rk(0, [128, 2, 2048], F32)
    wr_gt = rk(16384, [128, 2048], F32)
    wr_lt = rk(24576, [128, 2048], F32)
    fa_xq = rk(0, [128, 2, D], BF16)
    fa_xqT = RK[:, 2048:4096].rearrange("p (b c t) -> p b c t", b=2, c=DC)
    fa_p = rk(8192, [128, HF], F32)
    fa_sil = rk(24576, [128, 512], F32)
    fa_pq = rk(26624, [128, HF], BF16)
    fa_c32 = rk(34816, [128, 512], I32)
    fa_cf = rk(36864, [128, 1024], F32)
    fa_d = rk(40960, [128, 512], F32)
    fa_gt = rk(43008, [128, 512], F32)
    fb_x1 = rk(0, [128, 4, D], F32)
    fb_z = rk(16384, [128, D], F32)
    fb_zsq = rk(20480, [128, D], F32)
    fb_y = rk(24576, [128, 2, D], F32)

    row_a = rk(40960, [1, 256], F32)
    row_b = rk(41984, [1, 256], F32)
    row_c = rk(43008, [1, 256], F32)
    row_bc = rk(44032, [128, 256], F32)
    ones_col32 = nc.alloc_sbuf_tensor("ones_col32", [128, 1], F32)
    ones_col16 = nc.alloc_sbuf_tensor("ones_col16", [128, 1], BF16)
    ones_row32 = nc.alloc_sbuf_tensor("ones_row32", [1, 128], F32)
    id16_sb = nc.alloc_sbuf_tensor("id16_sb", [128, 128], BF16)
    bq_sb = nc.alloc_sbuf_tensor("bq_sb", [128, DC], F32)
    bk_sb = nc.alloc_sbuf_tensor("bk_sb", [128, DC], F32)
    b1_sb = nc.alloc_sbuf_tensor("b1_sb", [128, DC], F32)
    b2_sb = nc.alloc_sbuf_tensor("b2_sb", [128, HC], F32)
    col_sx = nc.alloc_sbuf_tensor("col_sx", [128, 32], F32)
    col_dqg = nc.alloc_sbuf_tensor("col_dqg", [128, 32], F32)
    col_rsu = nc.alloc_sbuf_tensor("col_rsu", [128, 32], F32)
    col_dqf = nc.alloc_sbuf_tensor("col_dqf", [128, 32], F32)
    col_am = nc.alloc_sbuf_tensor("col_am", [128, 1], F32)
    col_t0 = nc.alloc_sbuf_tensor("col_t0", [128, 1], F32)
    col_t1 = nc.alloc_sbuf_tensor("col_t1", [128, 1], F32)
    col_m = nc.alloc_sbuf_tensor("col_m", [128, 1], F32)
    col_v = nc.alloc_sbuf_tensor("col_v", [128, 1], F32)
    col_r = nc.alloc_sbuf_tensor("col_r", [128, 1], F32)
    col_pp = nc.alloc_sbuf_tensor("col_pp", [128, 8], F32)
    wsum = nc.alloc_sbuf_tensor("wsum", [128, 3], F32)
    thr_g = nc.alloc_sbuf_tensor("thr_g", [128, 1], F32)
    thr_gn = nc.alloc_sbuf_tensor("thr_gn", [128, 1], F32)
    thr_u = nc.alloc_sbuf_tensor("thr_u", [128, 1], F32)
    thr_un = nc.alloc_sbuf_tensor("thr_un", [128, 1], F32)
    thr_d = nc.alloc_sbuf_tensor("thr_d", [128, 1], F32)
    thr_dn = nc.alloc_sbuf_tensor("thr_dn", [128, 1], F32)
    md_g = nc.alloc_sbuf_tensor("md_g", [128, 1], F32)
    md_u = nc.alloc_sbuf_tensor("md_u", [128, 1], F32)
    md_d = nc.alloc_sbuf_tensor("md_d", [128, 1], F32)

    pb = [nc.alloc_psum_tensor(f"pb{i}", [128, 512], F32) for i in range(8)]
    PB_ROWS, PB_BC = 6, 7

    ctx = ExitStack()
    sem = {n: ctx.enter_context(nc.semaphore(n)) for n in
           ["sD", "sT", "sV", "sS", "sG"]}
    cnt = {n: 0 for n in sem}
    plan = {e: [] for e in ["tensor", "vector", "scalar", "sync", "gpsimd"]}

    def em(eng, fn):
        plan[eng].append(fn)

    def wt(eng, s, val=None):
        v = cnt[s] if val is None else val
        if v > 0:
            em(eng, lambda e, h=sem[s], v=v: e.wait_ge(h, v))
        return v

    def dma(out_ap, in_ap, transpose=False):
        cnt["sD"] += 16
        if transpose:
            em("sync", lambda e, o=out_ap, i=in_ap, h=sem["sD"]:
               e.dma_start_transpose(o, i).then_inc(h, 16))
        else:
            em("sync", lambda e, o=out_ap, i=in_ap, h=sem["sD"]:
               e.dma_start(o, i).then_inc(h, 16))
        return cnt["sD"]

    def grp(eng, s, emitter):
        cnt[s] += 1
        em(eng, lambda e, f=emitter, h=sem[s]: f(e).then_inc(h, 1))
        return cnt[s]

    bank_free = {i: ("sV", 0) for i in range(8)}

    def pe_wait_bank(i):
        s, v = bank_free[i]
        wt("tensor", s, v)

    def mm3(t, ps, lh, ll, rh, rl, first, last):
        t.matmul(ps, lh, rh, start=first, stop=False)
        t.matmul(ps, lh, rl, start=False, stop=False)
        return t.matmul(ps, ll, rh, start=False, stop=last)

    def split(v, src, hi, lo):
        v.tensor_copy(hi, src)
        return v.tensor_tensor(lo, src, hi, ALU.subtract)

    def rsqrt_newton_row(n, src_row, dst_row, tmp_row, guard=None):
        s1 = grp("scalar", "sS", lambda sE:
                 sE.activation(tmp_row[0:1, 0:n], src_row[0:1, 0:n], AF.Ln))
        wt("scalar", "sS", s1)
        stg = grp("scalar", "sS", lambda sE:
                  sE.activation(dst_row[0:1, 0:n], tmp_row[0:1, 0:n], AF.Exp,
                                scale=-0.5))
        wt("vector", "sS", stg)
        a, b_, s_ = dst_row[0:1, 0:n], tmp_row[0:1, 0:n], src_row[0:1, 0:n]
        ops = [
            lambda v: v.tensor_tensor(b_, a, a, ALU.mult),
            lambda v: v.tensor_tensor(b_, b_, s_, ALU.mult),
            lambda v: v.tensor_scalar(b_, b_, -0.5, 1.5, ALU.mult, ALU.add),
            lambda v: v.tensor_tensor(a, a, b_, ALU.mult),
        ]
        if guard is not None:
            ops.append(lambda v: v.tensor_scalar_min(a, a, guard))
        return vchain(*ops)

    def col_rsqrt_newton(src_col, dst_col, tmp_col, eps_bias):
        wt("vector", "sV")
        ve = grp("vector", "sV", lambda v:
                 v.tensor_scalar_add(src_col[...], src_col[...], eps_bias))
        wt("scalar", "sV", ve)
        s1 = grp("scalar", "sS", lambda sE:
                 sE.activation(tmp_col[...], src_col[...], AF.Ln))
        wt("scalar", "sS", s1)
        stg = grp("scalar", "sS", lambda sE:
                  sE.activation(dst_col[...], tmp_col[...], AF.Exp, scale=-0.5))
        wt("vector", "sS", stg)
        return vchain(
            lambda v: v.tensor_tensor(tmp_col[...], dst_col[...], dst_col[...], ALU.mult),
            lambda v: v.tensor_tensor(tmp_col[...], tmp_col[...], src_col[...], ALU.mult),
            lambda v: v.tensor_scalar(tmp_col[...], tmp_col[...], -0.5, 1.5, ALU.mult, ALU.add),
            lambda v: v.tensor_tensor(dst_col[...], dst_col[...], tmp_col[...], ALU.mult))

    def vchain(*emitters):
        last = 0
        for i, f in enumerate(emitters):
            wt("vector", "sV", last if i > 0 else None)
            last = grp("vector", "sV", f)
        return last

    def bcast_row(row_ap, n, dst_sbuf):
        pe_wait_bank(PB_BC)
        wt("tensor", "sV")
        ptg = grp("tensor", "sT", lambda t:
                  t.matmul(pb[PB_BC][:, 0:n], ones_row32[...], row_ap[0:1, 0:n],
                           start=True, stop=True))
        wt("vector", "sT", ptg)
        vtg = grp("vector", "sV", lambda v:
                  v.tensor_copy(dst_sbuf[:, 0:n], pb[PB_BC][:, 0:n]))
        bank_free[PB_BC] = ("sV", vtg)
        return vtg

    dma(bq_sb[...], d_bq[...])
    dma(id16_sb[...], d_id16[...])
    dma(bk_sb[...], d_bk[...])
    dma(b1_sb[...], d_b1[...])
    last_d = dma(b2_sb[...], d_b2[...])
    wt("vector", "sD", last_d)
    grp("vector", "sV", lambda v: (
        v.memset(ones_col32[...], 1.0),
        v.memset(ones_col16[...], 1.0),
        v.memset(ones_row32[...], 1.0))[-1])

    def build_bias_bcast(d_row, row_stage, dst_bc):
        wt("sync", "sV")
        tg = dma(row_stage[0:1, 0:D], d_row[...])
        for h in range(2):
            pe_wait_bank(PB_BC)
            wt("tensor", "sD", tg)
            wt("tensor", "sV")
            ptg = grp("tensor", "sT", lambda t, h=h:
                      t.matmul(pb[PB_BC][:, 0:512], ones_row32[...],
                               row_stage[0:1, h * 512:(h + 1) * 512],
                               start=True, stop=True))
            wt("vector", "sT", ptg)
            vtg = grp("vector", "sV", lambda v, h=h:
                      v.tensor_copy(dst_bc[:, h * 512:(h + 1) * 512],
                                    pb[PB_BC][:, 0:512]))
            bank_free[PB_BC] = ("sV", vtg)
        return cnt["sV"]

    def sync_barrier():
        wt("sync", "sT")
        wt("sync", "sV")
        wt("sync", "sS")
        wt("sync", "sD")

    def stream_weight_generic(d_wt, hi_view, lo_view, stage2, ncols, wslice=128):
        split_tgt = {}
        for q in range(ncols // wslice):
            c0 = q * wslice
            b = q % 2
            st = stage2[:, :, b * wslice:(b + 1) * wslice]
            if q >= 2:
                wt("sync", "sV", split_tgt[q - 2])
            tg = dma(st, d_wt[:, c0:c0 + wslice]
                     .rearrange("(c p) t -> p c t", p=128))
            wt("vector", "sD", tg)
            split_tgt[q] = grp("vector", "sV", lambda v, c0=c0, st=st: (
                v.tensor_copy(hi_view[:, :, c0:c0 + wslice], st),
                v.tensor_tensor(lo_view[:, :, c0:c0 + wslice], st,
                                hi_view[:, :, c0:c0 + wslice], ALU.subtract))[-1])
        return cnt["sV"]

    def stream_weight_Wc(d_wt, hi_view, lo_view):
        return stream_weight_generic(d_wt, hi_view, lo_view, xt_f, D)

    def proj_norm_tile(seq, t0, is_k, w_ready_V):
        width = TB if is_k else TQ
        wt("sync", "sV")
        tg = dma(xt_f[:, :, 0:width],
                 d_xT[seq, :, t0:t0 + width].rearrange("(c p) t -> p c t", p=128))
        wt("vector", "sD", tg)
        wt("vector", "sV", w_ready_V)
        vsp = grp("vector", "sV",
                  lambda v: split(v, xt_f[:, :, 0:width], xt_hi[:, :, 0:width],
                                  xt_lo[:, :, 0:width]))
        bias_sb = bk_sb if is_k else bq_sb
        for c in range(DC):
            bk_i = c % 2
            pe_wait_bank(bk_i)
            if c == 0:
                wt("tensor", "sV", vsp)

            def mms(t, c=c, bk_i=bk_i):
                for k in range(DC):
                    last = mm3(t, pb[bk_i][:, 0:width],
                               Wc_hi[:, k, c * 128:(c + 1) * 128],
                               Wc_lo[:, k, c * 128:(c + 1) * 128],
                               xt_hi[:, k, 0:width], xt_lo[:, k, 0:width],
                               k == 0, k == DC - 1)
                return last
            ptg = grp("tensor", "sT", mms)
            wt("scalar", "sT", ptg)
            wt("scalar", "sV")
            stg = grp("scalar", "sS", lambda sE, c=c, bk_i=bk_i:
                      sE.activation(qf[:, c, 0:width], pb[bk_i][:, 0:width],
                                    AF.Identity, bias=bias_sb[:, c:c + 1]))
            bank_free[bk_i] = ("sS", stg)
        wt("vector", "sS")
        vq = grp("vector", "sV", lambda v:
                 v.tensor_tensor(qsq[:, :, 0:width], qf[:, :, 0:width],
                                 qf[:, :, 0:width], ALU.mult))
        pe_wait_bank(PB_ROWS)
        wt("tensor", "sV", vq)

        def ssq_mm(t):
            for k in range(DC):
                last = t.matmul(pb[PB_ROWS][0:1, 0:width], ones_col32[...],
                                qsq[:, k, 0:width], start=(k == 0),
                                stop=(k == DC - 1))
            return last
        ptg = grp("tensor", "sT", ssq_mm)
        wt("scalar", "sT", ptg)
        wt("scalar", "sV")
        stg = grp("scalar", "sS", lambda sE:
                  sE.copy(row_a[0:1, 0:width], pb[PB_ROWS][0:1, 0:width]))
        bank_free[PB_ROWS] = ("sS", stg)
        wt("vector", "sS", stg)
        rsqrt_newton_row(width, row_a, row_b, row_c, guard=1.0 / EPS_NORM)
        bcast_row(row_b, width, row_bc)

        def norm_split(v):
            for c in range(DC):
                v.tensor_tensor(qf[:, c, 0:width], qf[:, c, 0:width],
                                row_bc[:, 0:width], ALU.mult)
            for c in range(DC):
                if is_k:
                    hi, lo = KnT_hi[:, c, t0:t0 + width], KnT_lo[:, c, t0:t0 + width]
                else:
                    hi, lo = qn_hi[:, c, 0:width], qn_lo[:, c, 0:width]
                v.tensor_copy(hi, qf[:, c, 0:width])
                last = v.tensor_tensor(lo, qf[:, c, 0:width], hi, ALU.subtract)
            return last
        return grp("vector", "sV", norm_split)

    def phase_kbuild(seq, w_ready):
        for it in range(S // TB):
            proj_norm_tile(seq, it * TB, True, w_ready)

    def phase_vbuild(seq, w_ready):
        build_bias_bcast(d_bv, rowd, bvbc)
        for it in range(S // TB):
            t0 = it * TB
            wt("sync", "sV")
            tg = dma(xt_f[...],
                     d_xT[seq, :, t0:t0 + TB].rearrange("(c p) t -> p c t", p=128))
            wt("vector", "sD", tg)
            wt("vector", "sV", w_ready)
            vsp = grp("vector", "sV",
                      lambda v: split(v, xt_f[...], xt_hi[...], xt_lo[...]))
            first = True
            for b in range(TB // 128):
                kc = it * (TB // 128) + b
                for dh in range(2):
                    bk_i = (b * 2 + dh) % 2
                    pe_wait_bank(bk_i)
                    if first:
                        wt("tensor", "sV", vsp)
                        first = False

                    def mms(t, b=b, dh=dh, bk_i=bk_i):
                        for k in range(DC):
                            last = mm3(t, pb[bk_i][:, 0:512],
                                       xt_hi[:, k, b * 128:(b + 1) * 128],
                                       xt_lo[:, k, b * 128:(b + 1) * 128],
                                       Wc_hi[:, k, dh * 512:(dh + 1) * 512],
                                       Wc_lo[:, k, dh * 512:(dh + 1) * 512],
                                       k == 0, k == DC - 1)
                        return last
                    ptg = grp("tensor", "sT", mms)
                    wt("vector", "sT", ptg)

                    def evac(v, kc=kc, dh=dh, bk_i=bk_i):
                        v.scalar_tensor_tensor(vb_scr[...], pb[bk_i][:, 0:512], 1.0,
                                               bvbc[:, dh * 512:(dh + 1) * 512],
                                               ALU.mult, ALU.add)
                        v.tensor_copy(V_hi[:, kc, dh * 512:(dh + 1) * 512], vb_scr[...])
                        return v.tensor_tensor(V_lo[:, kc, dh * 512:(dh + 1) * 512],
                                               vb_scr[...],
                                               V_hi[:, kc, dh * 512:(dh + 1) * 512],
                                               ALU.subtract)
                    vtg = grp("vector", "sV", evac)
                    bank_free[bk_i] = ("sV", vtg)

    def phase_attn(seq, w_ready):
        for qt in range(S // TQ):
            t0 = qt * TQ
            vn = proj_norm_tile(seq, t0, False, w_ready)
            pe_wait_bank(PB_ROWS)
            wt("tensor", "sV", vn)
            prev_lo = 0
            for kc in range(16):
                sb_i = kc % 2
                pe_wait_bank(sb_i)

                def smm(t, kc=kc, sb_i=sb_i):
                    for k in range(DC):
                        last = mm3(t, pb[sb_i][:, 0:TQ],
                                   KnT_hi[:, k, kc * 128:(kc + 1) * 128],
                                   KnT_lo[:, k, kc * 128:(kc + 1) * 128],
                                   qn_hi[:, k, 0:TQ], qn_lo[:, k, 0:TQ],
                                   k == 0, k == DC - 1)
                    return last
                ptg = grp("tensor", "sT", smm)
                wt("scalar", "sT", ptg)
                if prev_lo:
                    wt("scalar", "sV", prev_lo)
                stg = grp("scalar", "sS", lambda sE, kc=kc, sb_i=sb_i: (
                    sE.activation(e_f[...], pb[sb_i][:, 0:TQ], AF.Exp),
                    sE.activation(e_hi[:, kc, :], pb[sb_i][:, 0:TQ], AF.Exp))[-1])
                bank_free[sb_i] = ("sS", stg)
                wt("vector", "sS", stg)
                prev_lo = grp("vector", "sV", lambda v, kc=kc:
                              v.tensor_tensor(e_lo[:, kc, :], e_f[...],
                                              e_hi[:, kc, :], ALU.subtract))
                wt("tensor", "sV", prev_lo)

                def denom(t, kc=kc):
                    t.matmul(pb[PB_ROWS][0:1, 0:TQ], ones_col16[...],
                             e_hi[:, kc, :], start=(kc == 0), stop=False)
                    return t.matmul(pb[PB_ROWS][0:1, 0:TQ], ones_col16[...],
                                    e_lo[:, kc, :], start=False, stop=(kc == 15))
                grp("tensor", "sT", denom)
            wt("vector", "sT")
            wt("vector", "sV")
            vtg = vchain(
                lambda v: v.tensor_copy(row_a[0:1, 0:TQ], pb[PB_ROWS][0:1, 0:TQ]),
                lambda v: v.reciprocal(row_b[0:1, 0:TQ], row_a[0:1, 0:TQ]))
            bank_free[PB_ROWS] = ("sV", vtg)
            bcast_row(row_b, TQ, row_bc)
            rbc_v = cnt["sV"]
            for ds in range(DC):
                bk_i = 2 + (ds % 2)
                pe_wait_bank(bk_i)

                def avmm(t, ds=ds, bk_i=bk_i):
                    for kc in range(16):
                        t.matmul(pb[bk_i][:, 0:TQ],
                                 V_hi[:, kc, ds * 128:(ds + 1) * 128],
                                 e_hi[:, kc, :], start=(kc == 0), stop=False)
                        t.matmul(pb[bk_i][:, 0:TQ],
                                 V_hi[:, kc, ds * 128:(ds + 1) * 128],
                                 e_lo[:, kc, :], start=False, stop=False)
                        last = t.matmul(pb[bk_i][:, 0:TQ],
                                        V_lo[:, kc, ds * 128:(ds + 1) * 128],
                                        e_hi[:, kc, :], start=False,
                                        stop=(kc == 15))
                    return last
                ptg = grp("tensor", "sT", avmm)
                wt("vector", "sT", ptg)
                if ds == 0:
                    wt("vector", "sV", rbc_v)
                vtg = grp("vector", "sV", lambda v, ds=ds, bk_i=bk_i:
                          v.tensor_tensor(outf_ev[:, ds, :], pb[bk_i][:, 0:TQ],
                                          row_bc[:, 0:TQ], ALU.mult))
                bank_free[bk_i] = ("sV", vtg)
            wt("sync", "sV")
            dma(d_outf[:, seq * S + t0: seq * S + t0 + TQ]
                .rearrange("(c p) t -> p c t", p=128), outf_ev[...])

    w1_meanst = rk(0, [128, 2048], F32)

    def phase_w1(w_ready):
        h1f = w1_h1f
        h1hi_o, h1lo_o = qn_hi, qn_lo
        items = mean_items()
        grp("vector", "sV", lambda v: v.memset(wsum[...], 0.0))
        store_tgt = 0
        for it in range(T // TW):
            for _ in range(4):
                if items:
                    emit_mean_chunk(items.pop(0), w1_meanst)
            if it == T // TW - 1:
                assert not items, (len(items), 'mean chunks left')
            t0 = it * TW
            wt("sync", "sV")
            tg = dma(qf[...],
                     d_outf[:, t0:t0 + TW].rearrange("(c p) t -> p c t", p=128))
            wt("vector", "sD", tg)
            wt("vector", "sV", w_ready)
            vsp = grp("vector", "sV",
                      lambda v: split(v, qf[...], xt_hi[...], xt_lo[...]))
            for c in range(DC):
                bk_i = c % 2
                pe_wait_bank(bk_i)
                if c == 0:
                    wt("tensor", "sV", vsp)

                def mms(t, c=c, bk_i=bk_i):
                    for k in range(DC):
                        last = mm3(t, pb[bk_i][:, 0:TW],
                                   Wc_hi[:, k, c * 128:(c + 1) * 128],
                                   Wc_lo[:, k, c * 128:(c + 1) * 128],
                                   xt_hi[:, k, :], xt_lo[:, k, :],
                                   k == 0, k == DC - 1)
                    return last
                ptg = grp("tensor", "sT", mms)
                wt("scalar", "sT", ptg)
                wt("scalar", "sV")
                r = c % 2
                stg = grp("scalar", "sS", lambda sE, c=c, bk_i=bk_i, r=r:
                          sE.activation(h1f[:, r, :], pb[bk_i][:, 0:TW],
                                        AF.Relu, bias=b1_sb[:, c:c + 1]))
                wt("vector", "sS", stg)
                if c == 0:
                    wt("vector", "sD", store_tgt)
                vtg = grp("vector", "sV", lambda v, c=c, r=r:
                          split(v, h1f[:, r, :], h1hi_o[:, c, :], h1lo_o[:, c, :]))
                bank_free[bk_i] = ("sV", vtg)
            wt("sync", "sV")
            dma(d_h1h[:, t0:t0 + TW].rearrange("(c p) t -> p c t", p=128),
                h1hi_o[...])
            store_tgt = dma(d_h1l[:, t0:t0 + TW].rearrange("(c p) t -> p c t", p=128),
                            h1lo_o[...])

    def mean_items():
        items = []
        for d_wt, j in [(d_wgT, 0), (d_wuT, 1), (d_wdT, 2)]:
            ncol = d_wt.shape[1]
            nh = max(1, ncol // 2048)
            wc = ncol // nh
            for rchunk in range(d_wt.shape[0] // 128):
                for h in range(nh):
                    items.append((d_wt[rchunk * 128:(rchunk + 1) * 128,
                                       h * wc:(h + 1) * wc], j, wc))
        return items

    def emit_mean_chunk(item, stage):
        d_ap, j, wc = item
        wt("sync", "sV")
        tg = dma(stage[:, 0:wc], d_ap)
        wt("vector", "sD", tg)
        vchain(
            lambda v, wc=wc, stage=stage: v.reduce_sum(
                col_t0[...], stage[:, 0:wc], axis=AX.X,
                apply_absolute_value=True),
            lambda v, j=j: v.tensor_tensor(
                wsum[:, j:j + 1], wsum[:, j:j + 1], col_t0[...], ALU.add))

    def weight_mean_finalize():
        pe_wait_bank(PB_ROWS)
        wt("tensor", "sV")
        ptg = grp("tensor", "sT", lambda t:
                  t.matmul(pb[PB_ROWS][0:1, 0:3], ones_col32[...], wsum[...],
                           start=True, stop=True))
        wt("vector", "sT", ptg)
        vtg = vchain(
            lambda v: v.tensor_scalar_mul(row_a[0:1, 0:3], pb[PB_ROWS][0:1, 0:3],
                                          1.0 / (D * HF)),
            lambda v: v.tensor_scalar_max(row_a[0:1, 0:3], row_a[0:1, 0:3], EPS_Q),
            lambda v: v.tensor_scalar_mul(row_a[0:1, 3:6], row_a[0:1, 0:3], 0.5))
        bank_free[PB_ROWS] = ("sV", vtg)
        pe_wait_bank(PB_BC)
        wt("tensor", "sV", vtg)
        ptg = grp("tensor", "sT", lambda t:
                  t.matmul(pb[PB_BC][:, 0:6], ones_row32[...], row_a[0:1, 0:6],
                           start=True, stop=True))
        wt("vector", "sT", ptg)
        vtg = vchain(
            lambda v: (
                v.tensor_copy(md_g[...], pb[PB_BC][:, 0:1]),
                v.tensor_copy(md_u[...], pb[PB_BC][:, 1:2]),
                v.tensor_copy(md_d[...], pb[PB_BC][:, 2:3]),
                v.tensor_copy(thr_g[...], pb[PB_BC][:, 3:4]),
                v.tensor_copy(thr_u[...], pb[PB_BC][:, 4:5]),
                v.tensor_copy(thr_d[...], pb[PB_BC][:, 5:6]))[-1],
            lambda v: (
                v.tensor_scalar_mul(thr_gn[...], thr_g[...], -1.0),
                v.tensor_scalar_mul(thr_un[...], thr_u[...], -1.0),
                v.tensor_scalar_mul(thr_dn[...], thr_d[...], -1.0))[-1])
        bank_free[PB_BC] = ("sV", vtg)

    def round_chunk_emitters():
        items = []
        st_state = {"i": 0, "done": {}}

        def one(q):
            def emit():
                i = st_state["i"]
                b = i % 2
                if i >= 2:
                    wt("sync", "sV", st_state["done"][i - 2])
                tg = dma(wd_st[:, b, :], d_wdT[q * 128:(q + 1) * 128, :])
                wt("vector", "sD", tg)
                st_state["done"][i] = grp("vector", "sV", lambda v, q=q, b=b: (
                    v.tensor_scalar(wd_lt[...], wd_st[:, b, :], thr_dn[...],
                                    None, ALU.is_lt),
                    v.scalar_tensor_tensor(wdq8[:, q, :], wd_st[:, b, :],
                                           thr_d[...], wd_lt[...],
                                           ALU.is_gt, ALU.subtract))[-1])
                st_state["i"] = i + 1
            return emit
        for q in range(HC):
            items.append(one(q))
        return items

    def phase_w23(seq, round_items):
        a_banks = [2, 3, 4, 5]
        h1_tgt = {}
        build_bias_bcast(d_b3, rowd_w, b3bc)
        NC_ = 8
        CH = 512

        wst8a = w23_wst[:, :, 0:128]
        wst4 = rk(20480, [128, 4, 256], F32)

        def stage_weights(c):
            par = c % 2
            W2h, W2l, W3h, W3l = w23_views(par)
            ems = []
            for qq in range(4):
                c0 = qq * 128
                def em2(c=c, c0=c0, W2h=W2h, W2l=W2l):
                    wt("sync", "sV")
                    tg = dma(wst8a,
                             d_w2T[:, c * CH + c0: c * CH + c0 + 128]
                             .rearrange("(c p) t -> p c t", p=128))
                    wt("vector", "sD", tg)
                    grp("vector", "sV", lambda v: (
                        v.tensor_copy(W2h[:, :, c0:c0 + 128], wst8a),
                        v.tensor_tensor(W2l[:, :, c0:c0 + 128], wst8a,
                                        W2h[:, :, c0:c0 + 128],
                                        ALU.subtract))[-1])
                ems.append(em2)
            for qq in range(4):
                c0 = qq * 256
                def em3(c=c, c0=c0, W3h=W3h, W3l=W3l):
                    wt("sync", "sV")
                    tg = dma(wst4,
                             d_w3T[c * CH:(c + 1) * CH, c0:c0 + 256]
                             .rearrange("(c p) t -> p c t", p=128))
                    wt("vector", "sD", tg)
                    grp("vector", "sV", lambda v: (
                        v.tensor_copy(W3h[:, :, c0:c0 + 256], wst4),
                        v.tensor_tensor(W3l[:, :, c0:c0 + 256], wst4,
                                        W3h[:, :, c0:c0 + 256],
                                        ALU.subtract))[-1])
                ems.append(em3)
            return ems

        for em_ in stage_weights(0):
            em_()
        next_stage = []
        for c in range(NC_):
            par = c % 2
            W2h, W2l, W3h, W3l = w23_views(par)
            wrdy = cnt["sV"]
            if c + 1 < NC_:
                next_stage = stage_weights(c + 1)
            else:
                next_stage = []
            for it in range(S // TW):
                t0 = it * TW
                hb = it % 2
                if c == 0 and it == 0:
                    wt("sync", "sT")
                    dma(w23_h1h[:, 0],
                        d_h1h[:, seq * S: seq * S + TW]
                        .rearrange("(c p) t -> p c t", p=128))
                    h1_tgt[(0, 0)] = dma(
                        w23_h1l[:, 0],
                        d_h1l[:, seq * S: seq * S + TW]
                        .rearrange("(c p) t -> p c t", p=128))
                nxt = it + 1
                nc_, nit = (c, nxt) if nxt < S // TW else (c + 1, 0)
                if nc_ < NC_:
                    nt0 = nit * TW
                    wt("sync", "sT")
                    dma(w23_h1h[:, 1 - hb],
                        d_h1h[:, seq * S + nt0: seq * S + nt0 + TW]
                        .rearrange("(c p) t -> p c t", p=128))
                    h1_tgt[(nc_, nit)] = dma(
                        w23_h1l[:, 1 - hb],
                        d_h1l[:, seq * S + nt0: seq * S + nt0 + TW]
                        .rearrange("(c p) t -> p c t", p=128))
                wt("tensor", "sD", h1_tgt[(c, it)])
                wt("tensor", "sV", wrdy)
                for hs in range(4):
                    bk_i = hs % 2
                    pe_wait_bank(bk_i)

                    def h2mm(t, hs=hs, bk_i=bk_i, W2h=W2h, W2l=W2l, hb=hb):
                        for k in range(DC):
                            last = mm3(t, pb[bk_i][:, 0:TW],
                                       W2h[:, k, hs * 128:(hs + 1) * 128],
                                       W2l[:, k, hs * 128:(hs + 1) * 128],
                                       w23_h1h[:, hb, k, :], w23_h1l[:, hb, k, :],
                                       k == 0, k == DC - 1)
                        return last
                    ptg = grp("tensor", "sT", h2mm)
                    wt("scalar", "sT", ptg)
                    wt("scalar", "sV")
                    r = hs % 2
                    bidx = c * 4 + hs
                    stg = grp("scalar", "sS", lambda sE, bidx=bidx, bk_i=bk_i, r=r:
                              sE.activation(w23_h2f[:, r, :], pb[bk_i][:, 0:TW],
                                            AF.Relu,
                                            bias=b2_sb[:, bidx:bidx + 1]))
                    bank_free[bk_i] = ("sS", stg)
                    wt("vector", "sS", stg)
                    vtg = grp("vector", "sV", lambda v, r=r:
                              split(v, w23_h2f[:, r, :], w23_h2h[:, r, :],
                                    w23_h2l[:, r, :]))
                    if hs == 0:
                        for ab in a_banks:
                            pe_wait_bank(ab)
                    wt("tensor", "sV", vtg)

                    def w3mm(t, hs=hs, r=r, W3h=W3h, W3l=W3l):
                        for ts_ in range(2):
                            for dh in range(2):
                                ps = pb[a_banks[ts_ * 2 + dh]][:, 0:512]
                                last = mm3(t, ps,
                                           w23_h2h[:, r, ts_ * 128:(ts_ + 1) * 128],
                                           w23_h2l[:, r, ts_ * 128:(ts_ + 1) * 128],
                                           W3h[:, hs, dh * 512:(dh + 1) * 512],
                                           W3l[:, hs, dh * 512:(dh + 1) * 512],
                                           hs == 0, hs == 3)
                        return last
                    grp("tensor", "sT", w3mm)
                wt("vector", "sT")

                def a_evac(v, c=c, it=it):
                    for ts_ in range(2):
                        tc = it * 2 + ts_
                        for dh in range(2):
                            ps = pb[a_banks[ts_ * 2 + dh]][:, 0:512]
                            dst = a_acc[:, tc, dh * 512:(dh + 1) * 512]
                            if c == 0:
                                last = v.scalar_tensor_tensor(
                                    dst, ps, 1.0, b3bc[:, dh * 512:(dh + 1) * 512],
                                    ALU.mult, ALU.add)
                            else:
                                last = v.tensor_tensor(dst, dst, ps, ALU.add)
                    return last
                vtg = grp("vector", "sV", a_evac)
                for ab in a_banks:
                    bank_free[ab] = ("sV", vtg)
                if next_stage:
                    next_stage.pop(0)()
                if round_items and (c * 8 + it) % 4 == 1:
                    round_items.pop(0)()

    def phase_ln1(seq):
        store_guard = 0
        for tc in range(16):
            tile = seq * 16 + tc
            buf = tc % 2
            wt("sync", "sV")
            tg = dma(ln_x[:, buf, :], d_xtok[seq, tc * 128:(tc + 1) * 128, :])
            wt("vector", "sD", tg)
            vz = grp("vector", "sV", lambda v, tc=tc, buf=buf: (
                v.tensor_tensor(ln_z[...], ln_x[:, buf, :], a_acc[:, tc, :],
                                ALU.add),
                v.reduce_sum(col_m[...], ln_z[...], axis=AX.X))[-1])
            wt("scalar", "sV", vz)
            ssq = grp("scalar", "sS", lambda sE:
                      sE.activation(ln_zsq[...], ln_z[...], AF.Square))
            wt("vector", "sS", ssq)
            vchain(
                lambda v: v.reduce_sum(col_v[...], ln_zsq[...], axis=AX.X),
                lambda v: v.tensor_scalar_mul(col_m[...], col_m[...], 1.0 / D),
                lambda v: v.tensor_scalar_mul(col_v[...], col_v[...], 1.0 / D),
                lambda v: v.tensor_tensor(col_t0[...], col_m[...], col_m[...], ALU.mult),
                lambda v: v.tensor_tensor(col_v[...], col_v[...], col_t0[...],
                                          ALU.subtract))
            col_rsqrt_newton(col_v, col_r, col_t1, EPS_LN)
            if store_guard:
                wt("vector", "sD", store_guard)

            grp("vector", "sV", lambda v, buf=buf: (
                v.tensor_scalar(ln_x1[:, buf, :], ln_z[...], col_m[...], None,
                                ALU.subtract),
                v.tensor_scalar(ln_x1[:, buf, :], ln_x1[:, buf, :], col_r[...],
                                None, ALU.mult))[-1])
            vx1 = vchain(
                lambda v, buf=buf: v.reduce_max(col_am[...], ln_x1[:, buf, :],
                                                axis=AX.X,
                                                apply_absolute_value=True),
                lambda v: v.tensor_scalar_max(col_am[...], col_am[...], EPS_Q),
                lambda v: v.reciprocal(col_t0[...], col_am[...]),
                lambda v, tile=tile: (
                    v.tensor_scalar_mul(col_sx[:, tile:tile + 1], col_t0[...], 127.0),
                    v.tensor_tensor(col_t1[...], col_am[...], md_g[...], ALU.mult))[-1],
                lambda v, tile=tile: (
                    v.tensor_scalar_mul(col_dqg[:, tile:tile + 1], col_t1[...],
                                        1.0 / 127.0),
                    v.tensor_tensor(col_t1[...], col_am[...], md_u[...], ALU.mult))[-1],
                lambda v, tile=tile: v.tensor_scalar_mul(
                    col_rsu[:, tile:tile + 1], col_t1[...], 1.0 / 127.0))
            wt("sync", "sV", vx1)
            dma(d_x1[tile * 128:(tile + 1) * 128, :], ln_x1[:, buf, :])
            wt("scalar", "sV", vx1)
            sc = grp("scalar", "sS", lambda sE, buf=buf, tile=tile: (
                sE.activation(ln_c32[...], ln_x1[:, buf, :], AF.Copy,
                              scale=col_sx[:, tile:tile + 1]),
                sE.copy(ln_cf[...], ln_c32[...]))[-1])
            wt("vector", "sS", sc)

            def q_ops(v, buf=buf, tile=tile):
                v.scalar_tensor_tensor(ln_d[...], ln_x1[:, buf, :],
                                       col_sx[:, tile:tile + 1], ln_cf[...],
                                       ALU.mult, ALU.subtract)
                v.tensor_single_scalar(ln_gt[...], ln_d[...], 0.5, ALU.is_gt)
                v.tensor_single_scalar(ln_d[...], ln_d[...], -0.5, ALU.is_lt)
                v.tensor_tensor(ln_gt[...], ln_gt[...], ln_d[...], ALU.subtract)
                return v.tensor_tensor(ln_xq[:, buf, :], ln_cf[...], ln_gt[...],
                                       ALU.add)
            vq = grp("vector", "sV", q_ops)
            wt("sync", "sV", vq)
            store_guard = dma(d_xq[tile * 128:(tile + 1) * 128, :], ln_xq[:, buf, :])

    wround_state = {"i": 0, "done": {}}

    def weight_round(d_wt, dst_view, thr_p, thr_n):
        ncol = d_wt.shape[1]
        nh = max(1, ncol // 2048)
        w_cols = ncol // nh
        done_tgt = wround_state["done"]
        for q in range(d_wt.shape[0] // 128):
            for h in range(nh):
                i = wround_state["i"]
                b = i % 2
                if i >= 2:
                    wt("sync", "sV", done_tgt[i - 2])
                tg = dma(wr_st[:, b, 0:w_cols],
                         d_wt[q * 128:(q + 1) * 128, h * w_cols:(h + 1) * w_cols])
                wt("vector", "sD", tg)
                done_tgt[i] = grp("vector", "sV", lambda v, q=q, h=h, b=b: (
                    v.tensor_scalar(wr_lt[:, 0:w_cols], wr_st[:, b, 0:w_cols],
                                    thr_n[...], None, ALU.is_lt),
                    v.scalar_tensor_tensor(
                        dst_view[:, q, h * w_cols:(h + 1) * w_cols],
                        wr_st[:, b, 0:w_cols], thr_p[...], wr_lt[:, 0:w_cols],
                        ALU.is_gt, ALU.subtract))[-1])
                wround_state["i"] = i + 1
        return cnt["sV"]

    def phase_ffna(w_ready_V):
        pbT = [pb[i][...].bitcast(BF16) for i in range(8)]
        NT = T // 128
        load_tgt = {}
        tpose_done = {}
        pq_store = {}

        def load_xq(i):
            if i >= NT:
                return
            b = i % 2
            if i >= 2:
                wt("sync", "sT", tpose_done.get(i - 2, 0))
            load_tgt[i] = dma(fa_xq[:, b, :], d_xq[i * 128:(i + 1) * 128, :])

        def tpose_xq(i):
            if i >= NT:
                return
            b = i % 2
            wt("tensor", "sD", load_tgt[i])
            for k in range(DC):
                bk_i = 4 + k % 4
                pe_wait_bank(bk_i)
                ptg = grp("tensor", "sT", lambda t, k=k, b=b, bk_i=bk_i:
                          t.transpose(pbT[bk_i][:, 0:128],
                                      fa_xq[:, b, k * 128:(k + 1) * 128],
                                      id16_sb[...]))
                wt("scalar", "sT", ptg)
                stg = grp("scalar", "sS", lambda sE, k=k, b=b, bk_i=bk_i:
                          sE.copy(fa_xqT[:, b, k, :], pbT[bk_i][:, 0:128]))
                bank_free[bk_i] = ("sS", stg)
            tpose_done[i] = cnt["sT"]
            return cnt["sS"]

        load_xq(0)
        xqT_rdy = {0: tpose_xq(0)}
        load_xq(1)
        for tile in range(NT):
            b = tile % 2
            wt("tensor", "sS", xqT_rdy[tile])
            if tile == 0:
                wt("tensor", "sV", w_ready_V)
            sil_guard = 0
            for hsl in range(8):
                bg, bu = (hsl % 2) * 2, (hsl % 2) * 2 + 1
                pe_wait_bank(bg)
                pe_wait_bank(bu)

                def gumm(t, b=b, hsl=hsl, bg=bg, bu=bu):
                    for k in range(DC):
                        t.matmul(pb[bg][:, 0:512], fa_xqT[:, b, k, :],
                                 wgq[:, k, hsl * 512:(hsl + 1) * 512],
                                 start=(k == 0), stop=(k == DC - 1))
                    for k in range(DC):
                        last = t.matmul(pb[bu][:, 0:512], fa_xqT[:, b, k, :],
                                        wuq[:, k, hsl * 512:(hsl + 1) * 512],
                                        start=(k == 0), stop=(k == DC - 1))
                    return last
                ptg = grp("tensor", "sT", gumm)
                if hsl == 3:
                    load_xq(tile + 2)
                    xqT_rdy[tile + 1] = tpose_xq(tile + 1)
                wt("scalar", "sT", ptg)
                if sil_guard:
                    wt("scalar", "sV", sil_guard)
                stg = grp("scalar", "sS", lambda sE, tile=tile, bg=bg:
                          sE.activation(fa_sil[...], pb[bg][:, 0:512], AF.Silu,
                                        scale=col_dqg[:, tile:tile + 1]))
                bank_free[bg] = ("sS", stg)
                wt("vector", "sS", stg)
                if tile > 0 and hsl == 0:
                    wt("vector", "sD", pq_store.get(tile - 1, 0))
                vtg = grp("vector", "sV", lambda v, hsl=hsl, bu=bu: (
                    v.tensor_tensor(fa_p[:, hsl * 512:(hsl + 1) * 512],
                                    pb[bu][:, 0:512], fa_sil[...], ALU.mult),
                    v.reduce_max(col_pp[:, hsl:hsl + 1],
                                 fa_p[:, hsl * 512:(hsl + 1) * 512], axis=AX.X,
                                 apply_absolute_value=True))[-1]
                          )
                sil_guard = vtg
                bank_free[bu] = ("sV", vtg)
            vchain(
                lambda v: v.reduce_max(col_am[...], col_pp[...], axis=AX.X),
                lambda v, tile=tile: v.tensor_tensor(
                    col_am[...], col_am[...], col_rsu[:, tile:tile + 1],
                    ALU.mult),
                lambda v: v.tensor_scalar_max(col_am[...], col_am[...], EPS_Q),
                lambda v: v.reciprocal(col_t0[...], col_am[...]),
                lambda v: (
                    v.tensor_scalar_mul(col_t0[...], col_t0[...], 127.0),
                    v.tensor_tensor(col_t1[...], col_am[...], md_d[...],
                                    ALU.mult))[-1],
                lambda v, tile=tile: (
                    v.tensor_scalar_mul(col_dqf[:, tile:tile + 1], col_t1[...],
                                        1.0 / 127.0)),
                lambda v, tile=tile: v.tensor_tensor(
                    col_t0[...], col_t0[...], col_rsu[:, tile:tile + 1],
                    ALU.mult))
            sp_ready = cnt["sV"]
            for sl in range(8):
                rr = sl % 2
                cf = fa_cf[:, rr * 512:(rr + 1) * 512]
                if sl == 0:
                    wt("scalar", "sV", sp_ready)
                else:
                    wt("scalar", "sV")
                sc = grp("scalar", "sS", lambda sE, sl=sl, cf=cf: (
                    sE.activation(fa_c32[...], fa_p[:, sl * 512:(sl + 1) * 512],
                                  AF.Copy, scale=col_t0[...]),
                    sE.copy(cf, fa_c32[...]))[-1])
                wt("vector", "sS", sc)

                def qops(v, sl=sl, cf=cf):
                    v.scalar_tensor_tensor(fa_d[...],
                                           fa_p[:, sl * 512:(sl + 1) * 512],
                                           col_t0[...], cf, ALU.mult,
                                           ALU.subtract)
                    v.scalar_tensor_tensor(fa_gt[...], fa_d[...], 0.5,
                                           cf, ALU.is_gt, ALU.add)
                    v.tensor_single_scalar(fa_d[...], fa_d[...], -0.5, ALU.is_lt)
                    return v.tensor_tensor(fa_pq[:, sl * 512:(sl + 1) * 512],
                                           fa_gt[...], fa_d[...], ALU.subtract)
                grp("vector", "sV", qops)
            wt("sync", "sV")
            pq_store[tile] = dma(d_pq[tile * 128:(tile + 1) * 128, :], fa_pq[...])

    def phase_ffnb(upc_ready):
        y_store = {}
        half_done = {}
        half_tg = {}
        tgx = {}

        def load_half(g, half):
            if g > 0:
                wt("sync", "sT", half_done[(g - 1, half)])
            tg = 0
            for c_ in range(HC):
                tg = dma(pqT_g[:, c_, half * 512:(half + 1) * 512],
                         d_pq[g * GB + half * 512: g * GB + (half + 1) * 512,
                              c_ * 128:(c_ + 1) * 128],
                         transpose=True)
            half_tg[(g, half)] = tg

        wt("sync", "sT")
        load_half(0, 0)
        for g in range(T // GB):
            if g > 0:
                load_half(g, 0)
            wt("sync", "sV")
            for f in range(4):
                tile = g * (GB // 128) + f
                tgx[tile] = dma(fb_x1[:, f % 4, :],
                                d_x1[tile * 128:(tile + 1) * 128, :])
            load_half(g, 1)
            for f in range(GB // 128):
                tile = g * (GB // 128) + f
                xbuf = f % 4
                buf = f % 2
                wt("tensor", "sD", half_tg[(g, 0 if f < 4 else 1)])
                if g == 0 and f == 0:
                    wt("tensor", "sV", upc_ready)
                for dh in range(2):
                    pe_wait_bank(dh)

                    def dmm(t, f=f, dh=dh):
                        for hc in range(HC):
                            last = t.matmul(pb[dh][:, 0:512],
                                            pqT_g[:, hc, f * 128:(f + 1) * 128],
                                            wdq[:, hc, dh * 512:(dh + 1) * 512],
                                            start=(hc == 0), stop=(hc == HC - 1))
                        return last
                    grp("tensor", "sT", dmm)
                ptg = cnt["sT"]
                if f == 3:
                    half_done[(g, 0)] = ptg
                elif f == 7:
                    half_done[(g, 1)] = ptg
                wt("vector", "sT", ptg)
                wt("vector", "sD", tgx[tile])

                def z2ops(v, tile=tile, buf=xbuf):
                    for dh in range(2):
                        v.scalar_tensor_tensor(fb_z[:, dh * 512:(dh + 1) * 512],
                                               pb[dh][:, 0:512],
                                               col_dqf[:, tile:tile + 1],
                                               fb_x1[:, buf, dh * 512:(dh + 1) * 512],
                                               ALU.mult, ALU.add)
                    return v.reduce_sum(col_m[...], fb_z[...], axis=AX.X)
                vz = grp("vector", "sV", z2ops)
                bank_free[0] = ("sV", vz)
                bank_free[1] = ("sV", vz)
                if f < 4:
                    wt("sync", "sV", vz)
                    tgx[tile + 4] = dma(fb_x1[:, xbuf, :],
                                        d_x1[(tile + 4) * 128:(tile + 5) * 128, :])
                wt("scalar", "sV", vz)
                ssq = grp("scalar", "sS", lambda sE:
                          sE.activation(fb_zsq[...], fb_z[...], AF.Square))
                wt("vector", "sS", ssq)
                vchain(
                    lambda v: v.reduce_sum(col_v[...], fb_zsq[...], axis=AX.X),
                    lambda v: v.tensor_scalar_mul(col_m[...], col_m[...], 1.0 / D),
                    lambda v: v.tensor_scalar_mul(col_v[...], col_v[...], 1.0 / D),
                    lambda v: v.tensor_tensor(col_t0[...], col_m[...], col_m[...],
                                              ALU.mult),
                    lambda v: v.tensor_tensor(col_v[...], col_v[...], col_t0[...],
                                              ALU.subtract))
                col_rsqrt_newton(col_v, col_r, col_t1, EPS_LN)
                if tile >= 2:
                    wt("vector", "sD", y_store.get(tile - 2, 0))
                vy = grp("vector", "sV", lambda v, buf=buf: (
                    v.tensor_scalar(fb_y[:, buf, :], fb_z[...], col_m[...], None,
                                    ALU.subtract),
                    v.tensor_scalar(fb_y[:, buf, :], fb_y[:, buf, :], col_r[...],
                                    None, ALU.mult))[-1])
                wt("sync", "sV", vy)
                y_store[tile] = dma(d_y[tile * 128:(tile + 1) * 128, :],
                                    fb_y[:, buf, :])

    for seq in range(NSEQ):
        sync_barrier()
        wr = stream_weight_Wc(d_w["wkT"], Wc_hi, Wc_lo)
        phase_kbuild(seq, wr)
        sync_barrier()
        wr = stream_weight_Wc(d_w["wvT"], Wc_hi, Wc_lo)
        phase_vbuild(seq, wr)
        sync_barrier()
        wr = stream_weight_Wc(d_w["wqT"], Wc_hi, Wc_lo)
        phase_attn(seq, wr)
    sync_barrier()
    wr = stream_weight_Wc(d_w["w1T"], Wc_hi, Wc_lo)
    phase_w1(wr)
    sync_barrier()
    weight_mean_finalize()
    round_items = round_chunk_emitters()
    for seq in range(NSEQ):
        sync_barrier()
        phase_w23(seq, round_items)
        sync_barrier()
        phase_ln1(seq)
    sync_barrier()
    for em_ in round_items:
        em_()
    weight_round(d_wgT, wgq, thr_g, thr_gn)
    weight_round(d_wuT, wuq, thr_u, thr_un)
    gu_ready = cnt["sV"]
    sync_barrier()
    phase_ffna(gu_ready)
    sync_barrier()
    wt("scalar", "sT")
    wt("scalar", "sV")
    grp("scalar", "sS", lambda sE:
        sE.copy(RA[...], RCI8[:, 0:32768]))
    upc_s = cnt["sS"]
    wt("vector", "sS", upc_s)
    upc_v = grp("vector", "sV", lambda v: v.tensor_copy(col_t0[...], col_t0[...]))
    phase_ffnb(upc_v)
    em("sync", lambda e, h=sem["sD"], v=cnt["sD"]: e.wait_ge(h, v))

    with nc.Block() as block:
        @block.tensor
        def _(t):
            for fn in plan["tensor"]:
                fn(t)

        @block.vector
        def _(v):
            for fn in plan["vector"]:
                fn(v)

        @block.scalar
        def _(sE):
            for fn in plan["scalar"]:
                fn(sE)

        @block.gpsimd
        def _(g_):
            for fn in plan["gpsimd"]:
                fn(g_)

        @block.sync
        def _(sy):
            for fn in plan["sync"]:
                fn(sy)
    ctx.close()
    return nc


_NC_CACHE = []


def kernel(L0_attn_x, L0_attn_Wq, L0_attn_bq, L0_attn_Wk, L0_attn_bk,
           L0_attn_Wv, L0_attn_bv, L0_attn_W1, L0_attn_b1, L0_attn_W2,
           L0_attn_b2, L0_attn_W3, L0_attn_b3, L0_ffn_w_gate, L0_ffn_w_up,
           L0_ffn_w_down):
    f32 = lambda a: np.ascontiguousarray(np.asarray(a, dtype=np.float32))
    x = f32(L0_attn_x)
    wT = {
        "wqT": f32(L0_attn_Wq).T, "wkT": f32(L0_attn_Wk).T,
        "wvT": f32(L0_attn_Wv).T, "w1T": f32(L0_attn_W1).T,
        "w2T": f32(L0_attn_W2).T, "w3T": f32(L0_attn_W3).T,
        "wgT": f32(L0_ffn_w_gate).T, "wuT": f32(L0_ffn_w_up).T,
        "wdT": f32(L0_ffn_w_down).T,
    }
    wT = {k: np.ascontiguousarray(v) for k, v in wT.items()}
    bcol = lambda b, nchunk: np.ascontiguousarray(
        f32(b).reshape(nchunk, 128).T)
    shared = {
        **wT,
        "bq": bcol(L0_attn_bq, DC), "bk": bcol(L0_attn_bk, DC),
        "b1": bcol(L0_attn_b1, DC), "b2": bcol(L0_attn_b2, HC),
        "bvrow": f32(L0_attn_bv).reshape(1, D),
        "b3row": f32(L0_attn_b3).reshape(1, D),
        "ident16": np.eye(128, dtype=ml_dtypes.bfloat16),
    }
    if not _NC_CACHE:
        _NC_CACHE.append(build())
    nc = _NC_CACHE[0]
    in_maps = []
    for c in range(NCORES):
        xs = x[NSEQ * c: NSEQ * (c + 1)]
        in_maps.append({
            "xT": np.ascontiguousarray(xs.transpose(0, 2, 1)),
            "xtok": np.ascontiguousarray(xs),
            **shared,
        })
    res = run_bass_kernel_spmd(nc, in_maps, core_ids=list(range(NCORES)))
    out = np.concatenate([np.asarray(r["y"]).reshape(NSEQ, S, D)
                          for r in res.results], axis=0)
    return out


# version 16
# speedup vs baseline: 1.0220x; 1.0220x over previous
import numpy as np
import ml_dtypes
from contextlib import ExitStack
import concourse.bass as bass
import concourse.mybir as mybir
from concourse.bass_utils import run_bass_kernel_spmd

F32 = mybir.dt.float32
BF16 = mybir.dt.bfloat16
I32 = mybir.dt.int32
I8 = mybir.dt.int8
AF = mybir.ActivationFunctionType
ALU = mybir.AluOpType
AX = mybir.AxisListType

NCORES = 8
S, D, HF = 2048, 1024, 4096
NSEQ = 2
T = NSEQ * S
DC, HC = D // 128, HF // 128
TB = 256
TQ = 256
TW = 256
GA = 512
GB = 1024
EPS_LN = 1e-5
EPS_NORM = 1e-12
EPS_Q = 1e-5


def build():
    nc = bass.Bass("TRN2", target_bir_lowering=False, detect_race_conditions=False)

    d_xT = nc.dram_tensor("xT", [NSEQ, D, S], F32, kind="ExternalInput")
    d_xtok = nc.dram_tensor("xtok", [NSEQ, S, D], F32, kind="ExternalInput")
    d_w = {n: nc.dram_tensor(n, [D, D], F32, kind="ExternalInput")
           for n in ["wqT", "wkT", "wvT", "w1T"]}
    d_w2T = nc.dram_tensor("w2T", [D, HF], F32, kind="ExternalInput")
    d_w3T = nc.dram_tensor("w3T", [HF, D], F32, kind="ExternalInput")
    d_wgT = nc.dram_tensor("wgT", [D, HF], F32, kind="ExternalInput")
    d_wuT = nc.dram_tensor("wuT", [D, HF], F32, kind="ExternalInput")
    d_wdT = nc.dram_tensor("wdT", [HF, D], F32, kind="ExternalInput")
    d_bq = nc.dram_tensor("bq", [128, DC], F32, kind="ExternalInput")
    d_bk = nc.dram_tensor("bk", [128, DC], F32, kind="ExternalInput")
    d_b1 = nc.dram_tensor("b1", [128, DC], F32, kind="ExternalInput")
    d_b2 = nc.dram_tensor("b2", [128, HC], F32, kind="ExternalInput")
    d_bv = nc.dram_tensor("bvrow", [1, D], F32, kind="ExternalInput")
    d_b3 = nc.dram_tensor("b3row", [1, D], F32, kind="ExternalInput")
    d_id16 = nc.dram_tensor("ident16", [128, 128], BF16, kind="ExternalInput")
    d_y = nc.dram_tensor("y", [T, D], F32, kind="ExternalOutput")

    d_outf = nc.dram_tensor("scr_outf", [D, T], F32)
    d_h1h = nc.dram_tensor("scr_h1h", [D, T], BF16)
    d_h1l = nc.dram_tensor("scr_h1l", [D, T], BF16)
    d_x1 = nc.dram_tensor("scr_x1", [T, D], F32)
    d_xq = nc.dram_tensor("scr_xq", [T, D], BF16)
    d_pq = nc.dram_tensor("scr_pq", [T, HF], BF16)

    RA = nc.alloc_sbuf_tensor("RA", [128, 32768], BF16)
    RB = nc.alloc_sbuf_tensor("RB", [128, 32768], BF16)
    RC = nc.alloc_sbuf_tensor("RC", [128, 16384], BF16)
    RK = nc.alloc_sbuf_tensor("RK", [128, 22528], BF16)
    RA32, RB32, RK32 = RA.bitcast(F32), RB.bitcast(F32), RK.bitcast(F32)
    RKI32 = RK.bitcast(I32)
    RCI8 = RC.bitcast(I8)
    col_pp = None

    KnT_hi = RA[:, 0:16384].rearrange("p (c t) -> p c t", c=DC)
    KnT_lo = RA[:, 16384:32768].rearrange("p (c t) -> p c t", c=DC)
    V_hi = RB[:, 0:16384].rearrange("p (c d) -> p c d", c=16)
    V_lo = RB[:, 16384:32768].rearrange("p (c d) -> p c d", c=16)
    Wc_hi = RC[:, 0:8192].rearrange("p (c d) -> p c d", c=DC)
    Wc_lo = RC[:, 8192:16384].rearrange("p (c d) -> p c d", c=DC)
    def w23_views(par):
        base = par * 16384
        mk2 = lambda off: RA[:, base + off: base + off + 4096].rearrange(
            "p (c d) -> p c d", c=DC)
        mk3 = lambda off: RA[:, base + off: base + off + 4096].rearrange(
            "p (c d) -> p c d", c=4)
        return mk2(0), mk2(4096), mk3(8192), mk3(12288)
    RC32 = RC.bitcast(F32)
    wrc_st = RC32[:, 0:4096].rearrange("p (b t) -> p b t", b=2)
    wrc_gt = RC32[:, 4096:6144]
    wrc_lt = RC32[:, 6144:8192]
    a_acc = RB32.rearrange("p (c d) -> p c d", c=16)
    wgq = RA.rearrange("p (c h) -> p c h", c=DC)
    wuq = RB.rearrange("p (c h) -> p c h", c=DC)
    wdq8 = RCI8.rearrange("p (c d) -> p c d", c=HC)
    wdq = RA.rearrange("p (c d) -> p c d", c=HC)
    pqT_g = RB.rearrange("p (c t) -> p c t", c=HC)

    def rk(off, shape, dt=F32):
        n = int(np.prod(shape[1:]))
        sz = 2 if dt == BF16 else 4
        assert off % 4 == 0 and off + n * sz <= 45056, (off, n, sz)
        if dt == F32:
            base = RK32[:, off // 4: off // 4 + n]
        elif dt == BF16:
            base = RK[:, off // 2: off // 2 + n]
        elif dt == I32:
            base = RKI32[:, off // 4: off // 4 + n]
        if len(shape) == 3:
            return base.rearrange("p (a b) -> p a b", a=shape[1])
        if len(shape) == 4:
            return base.rearrange("p (a b c) -> p a b c", a=shape[1], b=shape[2])
        return base

    xt_f = rk(0, [128, DC, 256], F32)
    xt_hi = rk(8192, [128, DC, 256], BF16)
    xt_lo = rk(12288, [128, DC, 256], BF16)
    qf = rk(16384, [128, DC, 256], F32)
    vb_scr = rk(16384, [128, 512], F32)
    qsq = rk(24576, [128, DC, 256], F32)
    qn_hi = rk(32768, [128, DC, 256], BF16)
    qn_lo = rk(36864, [128, DC, 256], BF16)
    e_f = rk(16384, [128, 256], F32)
    e_hi = rk(24576, [128, 16, 256], BF16)
    e_lo = rk(0, [128, 16, 256], BF16)
    outf_ev = rk(16384, [128, DC, 256], F32)
    w1_h1f = rk(24576, [128, 2, 256], F32)
    bvbc = rk(32768, [128, D], F32)
    rowd = rk(36864, [1, D], F32)
    w23_h1h = rk(0, [128, 2, DC, TW], BF16)
    w23_h1l = rk(8192, [128, 2, DC, TW], BF16)
    w23_h2f = rk(16384, [128, 2, TW], F32)
    w23_h2h = rk(18432, [128, 2, TW], BF16)
    w23_h2l = rk(19456, [128, 2, TW], BF16)
    w23_wst = rk(20480, [128, DC, 256], F32)
    mean_st = rk(20480, [128, 4096], F32)
    wd_st = rk(28672, [128, 2, 1024], F32)
    wd_lt = rk(36864, [128, 1024], F32)
    b3bc = rk(40960, [128, D], F32)
    rowd_w = rk(36864, [1, D], F32)
    ln_x = rk(0, [128, 2, D], F32)
    ln_z = rk(8192, [128, D], F32)
    ln_zsq = rk(12288, [128, D], F32)
    ln_x1 = rk(16384, [128, 2, D], F32)
    ln_c32 = rk(24576, [128, D], I32)
    ln_cf = rk(28672, [128, D], F32)
    ln_d = rk(32768, [128, D], F32)
    ln_gt = rk(36864, [128, D], F32)
    ln_xq = rk(40960, [128, 2, D], BF16)
    wr_st = rk(0, [128, 2, 2048], F32)
    wr_gt = rk(16384, [128, 2048], F32)
    wr_lt = rk(24576, [128, 2048], F32)
    fa_xq = rk(0, [128, 2, D], BF16)
    fa_xqT = RK[:, 2048:4096].rearrange("p (b c t) -> p b c t", b=2, c=DC)
    fa_p = rk(8192, [128, HF], F32)
    fa_sil = rk(24576, [128, 512], F32)
    fa_pq = rk(26624, [128, HF], BF16)
    fa_c32 = rk(34816, [128, 512], I32)
    fa_cf = rk(36864, [128, 1024], F32)
    fa_d = rk(40960, [128, 512], F32)
    fa_gt = rk(43008, [128, 512], F32)
    fb_x1 = rk(0, [128, 4, D], F32)
    fb_z = rk(16384, [128, D], F32)
    fb_zsq = rk(20480, [128, D], F32)
    fb_y = rk(24576, [128, 2, D], F32)

    row_a = rk(40960, [1, 256], F32)
    row_b = rk(41984, [1, 256], F32)
    row_c = rk(43008, [1, 256], F32)
    row_bc = rk(44032, [128, 256], F32)
    ones_col32 = nc.alloc_sbuf_tensor("ones_col32", [128, 1], F32)
    ones_col16 = nc.alloc_sbuf_tensor("ones_col16", [128, 1], BF16)
    ones_row32 = nc.alloc_sbuf_tensor("ones_row32", [1, 128], F32)
    id16_sb = nc.alloc_sbuf_tensor("id16_sb", [128, 128], BF16)
    bq_sb = nc.alloc_sbuf_tensor("bq_sb", [128, DC], F32)
    bk_sb = nc.alloc_sbuf_tensor("bk_sb", [128, DC], F32)
    b1_sb = nc.alloc_sbuf_tensor("b1_sb", [128, DC], F32)
    b2_sb = nc.alloc_sbuf_tensor("b2_sb", [128, HC], F32)
    col_sx = nc.alloc_sbuf_tensor("col_sx", [128, 32], F32)
    col_dqg = nc.alloc_sbuf_tensor("col_dqg", [128, 32], F32)
    col_rsu = nc.alloc_sbuf_tensor("col_rsu", [128, 32], F32)
    col_dqf = nc.alloc_sbuf_tensor("col_dqf", [128, 32], F32)
    col_am = nc.alloc_sbuf_tensor("col_am", [128, 1], F32)
    col_t0 = nc.alloc_sbuf_tensor("col_t0", [128, 1], F32)
    col_t1 = nc.alloc_sbuf_tensor("col_t1", [128, 1], F32)
    col_m = nc.alloc_sbuf_tensor("col_m", [128, 1], F32)
    col_v = nc.alloc_sbuf_tensor("col_v", [128, 1], F32)
    col_r = nc.alloc_sbuf_tensor("col_r", [128, 1], F32)
    col_pp = nc.alloc_sbuf_tensor("col_pp", [128, 8], F32)
    wsum = nc.alloc_sbuf_tensor("wsum", [128, 3], F32)
    thr_g = nc.alloc_sbuf_tensor("thr_g", [128, 1], F32)
    thr_gn = nc.alloc_sbuf_tensor("thr_gn", [128, 1], F32)
    thr_u = nc.alloc_sbuf_tensor("thr_u", [128, 1], F32)
    thr_un = nc.alloc_sbuf_tensor("thr_un", [128, 1], F32)
    thr_d = nc.alloc_sbuf_tensor("thr_d", [128, 1], F32)
    thr_dn = nc.alloc_sbuf_tensor("thr_dn", [128, 1], F32)
    md_g = nc.alloc_sbuf_tensor("md_g", [128, 1], F32)
    md_u = nc.alloc_sbuf_tensor("md_u", [128, 1], F32)
    md_d = nc.alloc_sbuf_tensor("md_d", [128, 1], F32)

    pb = [nc.alloc_psum_tensor(f"pb{i}", [128, 512], F32) for i in range(8)]
    PB_ROWS, PB_BC = 6, 7

    ctx = ExitStack()
    sem = {n: ctx.enter_context(nc.semaphore(n)) for n in
           ["sD", "sT", "sV", "sS", "sG"]}
    cnt = {n: 0 for n in sem}
    plan = {e: [] for e in ["tensor", "vector", "scalar", "sync", "gpsimd"]}

    def em(eng, fn):
        plan[eng].append(fn)

    def wt(eng, s, val=None):
        v = cnt[s] if val is None else val
        if v > 0:
            em(eng, lambda e, h=sem[s], v=v: e.wait_ge(h, v))
        return v

    def dma(out_ap, in_ap, transpose=False):
        cnt["sD"] += 16
        if transpose:
            em("sync", lambda e, o=out_ap, i=in_ap, h=sem["sD"]:
               e.dma_start_transpose(o, i).then_inc(h, 16))
        else:
            em("sync", lambda e, o=out_ap, i=in_ap, h=sem["sD"]:
               e.dma_start(o, i).then_inc(h, 16))
        return cnt["sD"]

    def grp(eng, s, emitter):
        cnt[s] += 1
        em(eng, lambda e, f=emitter, h=sem[s]: f(e).then_inc(h, 1))
        return cnt[s]

    bank_free = {i: ("sV", 0) for i in range(8)}

    def pe_wait_bank(i):
        s, v = bank_free[i]
        wt("tensor", s, v)

    def mm3(t, ps, lh, ll, rh, rl, first, last):
        t.matmul(ps, lh, rh, start=first, stop=False)
        t.matmul(ps, lh, rl, start=False, stop=False)
        return t.matmul(ps, ll, rh, start=False, stop=last)

    def split(v, src, hi, lo):
        v.tensor_copy(hi, src)
        return v.tensor_tensor(lo, src, hi, ALU.subtract)

    def rsqrt_newton_row(n, src_row, dst_row, tmp_row, guard=None):
        s1 = grp("scalar", "sS", lambda sE:
                 sE.activation(tmp_row[0:1, 0:n], src_row[0:1, 0:n], AF.Ln))
        wt("scalar", "sS", s1)
        stg = grp("scalar", "sS", lambda sE:
                  sE.activation(dst_row[0:1, 0:n], tmp_row[0:1, 0:n], AF.Exp,
                                scale=-0.5))
        wt("vector", "sS", stg)
        a, b_, s_ = dst_row[0:1, 0:n], tmp_row[0:1, 0:n], src_row[0:1, 0:n]
        ops = [
            lambda v: v.tensor_tensor(b_, a, a, ALU.mult),
            lambda v: v.tensor_tensor(b_, b_, s_, ALU.mult),
            lambda v: v.tensor_scalar(b_, b_, -0.5, 1.5, ALU.mult, ALU.add),
            lambda v: v.tensor_tensor(a, a, b_, ALU.mult),
        ]
        if guard is not None:
            ops.append(lambda v: v.tensor_scalar_min(a, a, guard))
        return vchain(*ops)

    def col_rsqrt_newton(src_col, dst_col, tmp_col, eps_bias):
        wt("vector", "sV")
        ve = grp("vector", "sV", lambda v:
                 v.tensor_scalar_add(src_col[...], src_col[...], eps_bias))
        wt("scalar", "sV", ve)
        s1 = grp("scalar", "sS", lambda sE:
                 sE.activation(tmp_col[...], src_col[...], AF.Ln))
        wt("scalar", "sS", s1)
        stg = grp("scalar", "sS", lambda sE:
                  sE.activation(dst_col[...], tmp_col[...], AF.Exp, scale=-0.5))
        wt("vector", "sS", stg)
        return vchain(
            lambda v: v.tensor_tensor(tmp_col[...], dst_col[...], dst_col[...], ALU.mult),
            lambda v: v.tensor_tensor(tmp_col[...], tmp_col[...], src_col[...], ALU.mult),
            lambda v: v.tensor_scalar(tmp_col[...], tmp_col[...], -0.5, 1.5, ALU.mult, ALU.add),
            lambda v: v.tensor_tensor(dst_col[...], dst_col[...], tmp_col[...], ALU.mult))

    def vchain(*emitters):
        last = 0
        for i, f in enumerate(emitters):
            wt("vector", "sV", last if i > 0 else None)
            last = grp("vector", "sV", f)
        return last

    def bcast_row(row_ap, n, dst_sbuf):
        pe_wait_bank(PB_BC)
        wt("tensor", "sV")
        ptg = grp("tensor", "sT", lambda t:
                  t.matmul(pb[PB_BC][:, 0:n], ones_row32[...], row_ap[0:1, 0:n],
                           start=True, stop=True))
        wt("vector", "sT", ptg)
        vtg = grp("vector", "sV", lambda v:
                  v.tensor_copy(dst_sbuf[:, 0:n], pb[PB_BC][:, 0:n]))
        bank_free[PB_BC] = ("sV", vtg)
        return vtg

    dma(bq_sb[...], d_bq[...])
    dma(id16_sb[...], d_id16[...])
    dma(bk_sb[...], d_bk[...])
    dma(b1_sb[...], d_b1[...])
    last_d = dma(b2_sb[...], d_b2[...])
    wt("vector", "sD", last_d)
    grp("vector", "sV", lambda v: (
        v.memset(ones_col32[...], 1.0),
        v.memset(ones_col16[...], 1.0),
        v.memset(ones_row32[...], 1.0))[-1])

    def build_bias_bcast(d_row, row_stage, dst_bc):
        wt("sync", "sV")
        tg = dma(row_stage[0:1, 0:D], d_row[...])
        for h in range(2):
            pe_wait_bank(PB_BC)
            wt("tensor", "sD", tg)
            wt("tensor", "sV")
            ptg = grp("tensor", "sT", lambda t, h=h:
                      t.matmul(pb[PB_BC][:, 0:512], ones_row32[...],
                               row_stage[0:1, h * 512:(h + 1) * 512],
                               start=True, stop=True))
            wt("vector", "sT", ptg)
            vtg = grp("vector", "sV", lambda v, h=h:
                      v.tensor_copy(dst_bc[:, h * 512:(h + 1) * 512],
                                    pb[PB_BC][:, 0:512]))
            bank_free[PB_BC] = ("sV", vtg)
        return cnt["sV"]

    def sync_barrier():
        wt("sync", "sT")
        wt("sync", "sV")
        wt("sync", "sS")
        wt("sync", "sD")

    def stream_weight_generic(d_wt, hi_view, lo_view, stage2, ncols, wslice=128):
        split_tgt = {}
        for q in range(ncols // wslice):
            c0 = q * wslice
            b = q % 2
            st = stage2[:, :, b * wslice:(b + 1) * wslice]
            if q >= 2:
                wt("sync", "sV", split_tgt[q - 2])
            tg = dma(st, d_wt[:, c0:c0 + wslice]
                     .rearrange("(c p) t -> p c t", p=128))
            wt("vector", "sD", tg)
            split_tgt[q] = grp("vector", "sV", lambda v, c0=c0, st=st: (
                v.tensor_copy(hi_view[:, :, c0:c0 + wslice], st),
                v.tensor_tensor(lo_view[:, :, c0:c0 + wslice], st,
                                hi_view[:, :, c0:c0 + wslice], ALU.subtract))[-1])
        return cnt["sV"]

    def stream_weight_Wc(d_wt, hi_view, lo_view):
        return stream_weight_generic(d_wt, hi_view, lo_view, xt_f, D)

    def proj_norm_tile(seq, t0, is_k, w_ready_V):
        width = TB if is_k else TQ
        wt("sync", "sV")
        tg = dma(xt_f[:, :, 0:width],
                 d_xT[seq, :, t0:t0 + width].rearrange("(c p) t -> p c t", p=128))
        wt("vector", "sD", tg)
        wt("vector", "sV", w_ready_V)
        vsp = grp("vector", "sV",
                  lambda v: split(v, xt_f[:, :, 0:width], xt_hi[:, :, 0:width],
                                  xt_lo[:, :, 0:width]))
        bias_sb = bk_sb if is_k else bq_sb
        for c in range(DC):
            bk_i = c % 2
            pe_wait_bank(bk_i)
            if c == 0:
                wt("tensor", "sV", vsp)

            def mms(t, c=c, bk_i=bk_i):
                for k in range(DC):
                    last = mm3(t, pb[bk_i][:, 0:width],
                               Wc_hi[:, k, c * 128:(c + 1) * 128],
                               Wc_lo[:, k, c * 128:(c + 1) * 128],
                               xt_hi[:, k, 0:width], xt_lo[:, k, 0:width],
                               k == 0, k == DC - 1)
                return last
            ptg = grp("tensor", "sT", mms)
            wt("scalar", "sT", ptg)
            wt("scalar", "sV")
            stg = grp("scalar", "sS", lambda sE, c=c, bk_i=bk_i:
                      sE.activation(qf[:, c, 0:width], pb[bk_i][:, 0:width],
                                    AF.Identity, bias=bias_sb[:, c:c + 1]))
            bank_free[bk_i] = ("sS", stg)
        wt("vector", "sS")
        vq = grp("vector", "sV", lambda v:
                 v.tensor_tensor(qsq[:, :, 0:width], qf[:, :, 0:width],
                                 qf[:, :, 0:width], ALU.mult))
        pe_wait_bank(PB_ROWS)
        wt("tensor", "sV", vq)

        def ssq_mm(t):
            for k in range(DC):
                last = t.matmul(pb[PB_ROWS][0:1, 0:width], ones_col32[...],
                                qsq[:, k, 0:width], start=(k == 0),
                                stop=(k == DC - 1))
            return last
        ptg = grp("tensor", "sT", ssq_mm)
        wt("scalar", "sT", ptg)
        wt("scalar", "sV")
        stg = grp("scalar", "sS", lambda sE:
                  sE.copy(row_a[0:1, 0:width], pb[PB_ROWS][0:1, 0:width]))
        bank_free[PB_ROWS] = ("sS", stg)
        wt("vector", "sS", stg)
        rsqrt_newton_row(width, row_a, row_b, row_c, guard=1.0 / EPS_NORM)
        bcast_row(row_b, width, row_bc)

        def norm_split(v):
            for c in range(DC):
                v.tensor_tensor(qf[:, c, 0:width], qf[:, c, 0:width],
                                row_bc[:, 0:width], ALU.mult)
            for c in range(DC):
                if is_k:
                    hi, lo = KnT_hi[:, c, t0:t0 + width], KnT_lo[:, c, t0:t0 + width]
                else:
                    hi, lo = qn_hi[:, c, 0:width], qn_lo[:, c, 0:width]
                v.tensor_copy(hi, qf[:, c, 0:width])
                last = v.tensor_tensor(lo, qf[:, c, 0:width], hi, ALU.subtract)
            return last
        return grp("vector", "sV", norm_split)

    def phase_kbuild(seq, w_ready):
        for it in range(S // TB):
            proj_norm_tile(seq, it * TB, True, w_ready)

    def phase_vbuild(seq, w_ready):
        build_bias_bcast(d_bv, rowd, bvbc)
        for it in range(S // TB):
            t0 = it * TB
            wt("sync", "sV")
            tg = dma(xt_f[...],
                     d_xT[seq, :, t0:t0 + TB].rearrange("(c p) t -> p c t", p=128))
            wt("vector", "sD", tg)
            wt("vector", "sV", w_ready)
            vsp = grp("vector", "sV",
                      lambda v: split(v, xt_f[...], xt_hi[...], xt_lo[...]))
            first = True
            for b in range(TB // 128):
                kc = it * (TB // 128) + b
                for dh in range(2):
                    bk_i = (b * 2 + dh) % 2
                    pe_wait_bank(bk_i)
                    if first:
                        wt("tensor", "sV", vsp)
                        first = False

                    def mms(t, b=b, dh=dh, bk_i=bk_i):
                        for k in range(DC):
                            last = mm3(t, pb[bk_i][:, 0:512],
                                       xt_hi[:, k, b * 128:(b + 1) * 128],
                                       xt_lo[:, k, b * 128:(b + 1) * 128],
                                       Wc_hi[:, k, dh * 512:(dh + 1) * 512],
                                       Wc_lo[:, k, dh * 512:(dh + 1) * 512],
                                       k == 0, k == DC - 1)
                        return last
                    ptg = grp("tensor", "sT", mms)
                    wt("vector", "sT", ptg)

                    def evac(v, kc=kc, dh=dh, bk_i=bk_i):
                        v.scalar_tensor_tensor(vb_scr[...], pb[bk_i][:, 0:512], 1.0,
                                               bvbc[:, dh * 512:(dh + 1) * 512],
                                               ALU.mult, ALU.add)
                        v.tensor_copy(V_hi[:, kc, dh * 512:(dh + 1) * 512], vb_scr[...])
                        return v.tensor_tensor(V_lo[:, kc, dh * 512:(dh + 1) * 512],
                                               vb_scr[...],
                                               V_hi[:, kc, dh * 512:(dh + 1) * 512],
                                               ALU.subtract)
                    vtg = grp("vector", "sV", evac)
                    bank_free[bk_i] = ("sV", vtg)

    def phase_attn(seq, w_ready):
        for qt in range(S // TQ):
            t0 = qt * TQ
            vn = proj_norm_tile(seq, t0, False, w_ready)
            pe_wait_bank(PB_ROWS)
            wt("tensor", "sV", vn)
            prev_lo = 0
            for kc in range(16):
                sb_i = kc % 2
                pe_wait_bank(sb_i)

                def smm(t, kc=kc, sb_i=sb_i):
                    for k in range(DC):
                        last = mm3(t, pb[sb_i][:, 0:TQ],
                                   KnT_hi[:, k, kc * 128:(kc + 1) * 128],
                                   KnT_lo[:, k, kc * 128:(kc + 1) * 128],
                                   qn_hi[:, k, 0:TQ], qn_lo[:, k, 0:TQ],
                                   k == 0, k == DC - 1)
                    return last
                ptg = grp("tensor", "sT", smm)
                wt("scalar", "sT", ptg)
                if prev_lo:
                    wt("scalar", "sV", prev_lo)
                stg = grp("scalar", "sS", lambda sE, kc=kc, sb_i=sb_i: (
                    sE.activation(e_f[...], pb[sb_i][:, 0:TQ], AF.Exp),
                    sE.activation(e_hi[:, kc, :], pb[sb_i][:, 0:TQ], AF.Exp))[-1])
                bank_free[sb_i] = ("sS", stg)
                wt("vector", "sS", stg)
                prev_lo = grp("vector", "sV", lambda v, kc=kc:
                              v.tensor_tensor(e_lo[:, kc, :], e_f[...],
                                              e_hi[:, kc, :], ALU.subtract))
                wt("tensor", "sV", prev_lo)

                def denom(t, kc=kc):
                    t.matmul(pb[PB_ROWS][0:1, 0:TQ], ones_col16[...],
                             e_hi[:, kc, :], start=(kc == 0), stop=False)
                    return t.matmul(pb[PB_ROWS][0:1, 0:TQ], ones_col16[...],
                                    e_lo[:, kc, :], start=False, stop=(kc == 15))
                grp("tensor", "sT", denom)
            wt("vector", "sT")
            wt("vector", "sV")
            vtg = vchain(
                lambda v: v.tensor_copy(row_a[0:1, 0:TQ], pb[PB_ROWS][0:1, 0:TQ]),
                lambda v: v.reciprocal(row_b[0:1, 0:TQ], row_a[0:1, 0:TQ]))
            bank_free[PB_ROWS] = ("sV", vtg)
            bcast_row(row_b, TQ, row_bc)
            rbc_v = cnt["sV"]
            for ds in range(DC):
                bk_i = 2 + (ds % 2)
                pe_wait_bank(bk_i)

                def avmm(t, ds=ds, bk_i=bk_i):
                    for kc in range(16):
                        t.matmul(pb[bk_i][:, 0:TQ],
                                 V_hi[:, kc, ds * 128:(ds + 1) * 128],
                                 e_hi[:, kc, :], start=(kc == 0), stop=False)
                        t.matmul(pb[bk_i][:, 0:TQ],
                                 V_hi[:, kc, ds * 128:(ds + 1) * 128],
                                 e_lo[:, kc, :], start=False, stop=False)
                        last = t.matmul(pb[bk_i][:, 0:TQ],
                                        V_lo[:, kc, ds * 128:(ds + 1) * 128],
                                        e_hi[:, kc, :], start=False,
                                        stop=(kc == 15))
                    return last
                ptg = grp("tensor", "sT", avmm)
                wt("vector", "sT", ptg)
                if ds == 0:
                    wt("vector", "sV", rbc_v)
                vtg = grp("vector", "sV", lambda v, ds=ds, bk_i=bk_i:
                          v.tensor_tensor(outf_ev[:, ds, :], pb[bk_i][:, 0:TQ],
                                          row_bc[:, 0:TQ], ALU.mult))
                bank_free[bk_i] = ("sV", vtg)
            wt("sync", "sV")
            dma(d_outf[:, seq * S + t0: seq * S + t0 + TQ]
                .rearrange("(c p) t -> p c t", p=128), outf_ev[...])

    w1_meanst = rk(0, [128, 2048], F32)

    def phase_w1(w_ready):
        h1f = w1_h1f
        h1hi_o, h1lo_o = qn_hi, qn_lo
        items = mean_items()
        grp("vector", "sV", lambda v: v.memset(wsum[...], 0.0))
        store_tgt = 0
        for it in range(T // TW):
            for _ in range(4):
                if items:
                    emit_mean_chunk(items.pop(0), w1_meanst)
            if it == T // TW - 1:
                assert not items, (len(items), 'mean chunks left')
            t0 = it * TW
            wt("sync", "sV")
            tg = dma(qf[...],
                     d_outf[:, t0:t0 + TW].rearrange("(c p) t -> p c t", p=128))
            wt("vector", "sD", tg)
            wt("vector", "sV", w_ready)
            vsp = grp("vector", "sV",
                      lambda v: split(v, qf[...], xt_hi[...], xt_lo[...]))
            for c in range(DC):
                bk_i = c % 2
                pe_wait_bank(bk_i)
                if c == 0:
                    wt("tensor", "sV", vsp)

                def mms(t, c=c, bk_i=bk_i):
                    for k in range(DC):
                        last = mm3(t, pb[bk_i][:, 0:TW],
                                   Wc_hi[:, k, c * 128:(c + 1) * 128],
                                   Wc_lo[:, k, c * 128:(c + 1) * 128],
                                   xt_hi[:, k, :], xt_lo[:, k, :],
                                   k == 0, k == DC - 1)
                    return last
                ptg = grp("tensor", "sT", mms)
                wt("scalar", "sT", ptg)
                wt("scalar", "sV")
                r = c % 2
                stg = grp("scalar", "sS", lambda sE, c=c, bk_i=bk_i, r=r:
                          sE.activation(h1f[:, r, :], pb[bk_i][:, 0:TW],
                                        AF.Relu, bias=b1_sb[:, c:c + 1]))
                wt("vector", "sS", stg)
                if c == 0:
                    wt("vector", "sD", store_tgt)
                vtg = grp("vector", "sV", lambda v, c=c, r=r:
                          split(v, h1f[:, r, :], h1hi_o[:, c, :], h1lo_o[:, c, :]))
                bank_free[bk_i] = ("sV", vtg)
            wt("sync", "sV")
            dma(d_h1h[:, t0:t0 + TW].rearrange("(c p) t -> p c t", p=128),
                h1hi_o[...])
            store_tgt = dma(d_h1l[:, t0:t0 + TW].rearrange("(c p) t -> p c t", p=128),
                            h1lo_o[...])

    def mean_items():
        items = []
        for d_wt, j in [(d_wgT, 0), (d_wuT, 1), (d_wdT, 2)]:
            ncol = d_wt.shape[1]
            nh = max(1, ncol // 2048)
            wc = ncol // nh
            for rchunk in range(d_wt.shape[0] // 128):
                for h in range(nh):
                    items.append((d_wt[rchunk * 128:(rchunk + 1) * 128,
                                       h * wc:(h + 1) * wc], j, wc))
        return items

    def emit_mean_chunk(item, stage):
        d_ap, j, wc = item
        wt("sync", "sV")
        tg = dma(stage[:, 0:wc], d_ap)
        wt("vector", "sD", tg)
        vchain(
            lambda v, wc=wc, stage=stage: v.reduce_sum(
                col_t0[...], stage[:, 0:wc], axis=AX.X,
                apply_absolute_value=True),
            lambda v, j=j: v.tensor_tensor(
                wsum[:, j:j + 1], wsum[:, j:j + 1], col_t0[...], ALU.add))

    def weight_mean_finalize():
        pe_wait_bank(PB_ROWS)
        wt("tensor", "sV")
        ptg = grp("tensor", "sT", lambda t:
                  t.matmul(pb[PB_ROWS][0:1, 0:3], ones_col32[...], wsum[...],
                           start=True, stop=True))
        wt("vector", "sT", ptg)
        vtg = vchain(
            lambda v: v.tensor_scalar_mul(row_a[0:1, 0:3], pb[PB_ROWS][0:1, 0:3],
                                          1.0 / (D * HF)),
            lambda v: v.tensor_scalar_max(row_a[0:1, 0:3], row_a[0:1, 0:3], EPS_Q),
            lambda v: v.tensor_scalar_mul(row_a[0:1, 3:6], row_a[0:1, 0:3], 0.5))
        bank_free[PB_ROWS] = ("sV", vtg)
        pe_wait_bank(PB_BC)
        wt("tensor", "sV", vtg)
        ptg = grp("tensor", "sT", lambda t:
                  t.matmul(pb[PB_BC][:, 0:6], ones_row32[...], row_a[0:1, 0:6],
                           start=True, stop=True))
        wt("vector", "sT", ptg)
        vtg = vchain(
            lambda v: (
                v.tensor_copy(md_g[...], pb[PB_BC][:, 0:1]),
                v.tensor_copy(md_u[...], pb[PB_BC][:, 1:2]),
                v.tensor_copy(md_d[...], pb[PB_BC][:, 2:3]),
                v.tensor_copy(thr_g[...], pb[PB_BC][:, 3:4]),
                v.tensor_copy(thr_u[...], pb[PB_BC][:, 4:5]),
                v.tensor_copy(thr_d[...], pb[PB_BC][:, 5:6]))[-1],
            lambda v: (
                v.tensor_scalar_mul(thr_gn[...], thr_g[...], -1.0),
                v.tensor_scalar_mul(thr_un[...], thr_u[...], -1.0),
                v.tensor_scalar_mul(thr_dn[...], thr_d[...], -1.0))[-1])
        bank_free[PB_BC] = ("sV", vtg)

    def round_chunk_emitters():
        items = []
        st_state = {"i": 0, "done": {}}

        def one(q):
            def emit():
                i = st_state["i"]
                b = i % 2
                if i >= 2:
                    wt("sync", "sV", st_state["done"][i - 2])
                tg = dma(wd_st[:, b, :], d_wdT[q * 128:(q + 1) * 128, :])
                wt("vector", "sD", tg)
                st_state["done"][i] = grp("vector", "sV", lambda v, q=q, b=b: (
                    v.tensor_scalar(wd_lt[...], wd_st[:, b, :], thr_dn[...],
                                    None, ALU.is_lt),
                    v.scalar_tensor_tensor(wdq8[:, q, :], wd_st[:, b, :],
                                           thr_d[...], wd_lt[...],
                                           ALU.is_gt, ALU.subtract))[-1])
                st_state["i"] = i + 1
            return emit
        for q in range(HC):
            items.append(one(q))
        return items

    def phase_w23(seq, round_items):
        a_banks = [2, 3, 4, 5]
        h1_tgt = {}
        build_bias_bcast(d_b3, rowd_w, b3bc)
        NC_ = 8
        CH = 512

        wst8a = w23_wst[:, :, 0:128]
        wst4 = rk(20480, [128, 4, 256], F32)

        def stage_weights(c):
            par = c % 2
            W2h, W2l, W3h, W3l = w23_views(par)
            ems = []
            for qq in range(4):
                c0 = qq * 128
                def em2(c=c, c0=c0, W2h=W2h, W2l=W2l):
                    wt("sync", "sV")
                    tg = dma(wst8a,
                             d_w2T[:, c * CH + c0: c * CH + c0 + 128]
                             .rearrange("(c p) t -> p c t", p=128))
                    wt("vector", "sD", tg)
                    grp("vector", "sV", lambda v: (
                        v.tensor_copy(W2h[:, :, c0:c0 + 128], wst8a),
                        v.tensor_tensor(W2l[:, :, c0:c0 + 128], wst8a,
                                        W2h[:, :, c0:c0 + 128],
                                        ALU.subtract))[-1])
                ems.append(em2)
            for qq in range(4):
                c0 = qq * 256
                def em3(c=c, c0=c0, W3h=W3h, W3l=W3l):
                    wt("sync", "sV")
                    tg = dma(wst4,
                             d_w3T[c * CH:(c + 1) * CH, c0:c0 + 256]
                             .rearrange("(c p) t -> p c t", p=128))
                    wt("vector", "sD", tg)
                    grp("vector", "sV", lambda v: (
                        v.tensor_copy(W3h[:, :, c0:c0 + 256], wst4),
                        v.tensor_tensor(W3l[:, :, c0:c0 + 256], wst4,
                                        W3h[:, :, c0:c0 + 256],
                                        ALU.subtract))[-1])
                ems.append(em3)
            return ems

        for em_ in stage_weights(0):
            em_()
        next_stage = []
        for c in range(NC_):
            par = c % 2
            W2h, W2l, W3h, W3l = w23_views(par)
            wrdy = cnt["sV"]
            if c + 1 < NC_:
                next_stage = stage_weights(c + 1)
            else:
                next_stage = []
            for it in range(S // TW):
                t0 = it * TW
                hb = it % 2
                if c == 0 and it == 0:
                    wt("sync", "sT")
                    dma(w23_h1h[:, 0],
                        d_h1h[:, seq * S: seq * S + TW]
                        .rearrange("(c p) t -> p c t", p=128))
                    h1_tgt[(0, 0)] = dma(
                        w23_h1l[:, 0],
                        d_h1l[:, seq * S: seq * S + TW]
                        .rearrange("(c p) t -> p c t", p=128))
                nxt = it + 1
                nc_, nit = (c, nxt) if nxt < S // TW else (c + 1, 0)
                if nc_ < NC_:
                    nt0 = nit * TW
                    wt("sync", "sT")
                    dma(w23_h1h[:, 1 - hb],
                        d_h1h[:, seq * S + nt0: seq * S + nt0 + TW]
                        .rearrange("(c p) t -> p c t", p=128))
                    h1_tgt[(nc_, nit)] = dma(
                        w23_h1l[:, 1 - hb],
                        d_h1l[:, seq * S + nt0: seq * S + nt0 + TW]
                        .rearrange("(c p) t -> p c t", p=128))
                wt("tensor", "sD", h1_tgt[(c, it)])
                wt("tensor", "sV", wrdy)
                for hs in range(4):
                    bk_i = hs % 2
                    pe_wait_bank(bk_i)

                    def h2mm(t, hs=hs, bk_i=bk_i, W2h=W2h, W2l=W2l, hb=hb):
                        for k in range(DC):
                            last = mm3(t, pb[bk_i][:, 0:TW],
                                       W2h[:, k, hs * 128:(hs + 1) * 128],
                                       W2l[:, k, hs * 128:(hs + 1) * 128],
                                       w23_h1h[:, hb, k, :], w23_h1l[:, hb, k, :],
                                       k == 0, k == DC - 1)
                        return last
                    ptg = grp("tensor", "sT", h2mm)
                    wt("scalar", "sT", ptg)
                    wt("scalar", "sV")
                    r = hs % 2
                    bidx = c * 4 + hs
                    stg = grp("scalar", "sS", lambda sE, bidx=bidx, bk_i=bk_i, r=r:
                              sE.activation(w23_h2f[:, r, :], pb[bk_i][:, 0:TW],
                                            AF.Relu,
                                            bias=b2_sb[:, bidx:bidx + 1]))
                    bank_free[bk_i] = ("sS", stg)
                    wt("vector", "sS", stg)
                    vtg = grp("vector", "sV", lambda v, r=r:
                              split(v, w23_h2f[:, r, :], w23_h2h[:, r, :],
                                    w23_h2l[:, r, :]))
                    if hs == 0:
                        for ab in a_banks:
                            pe_wait_bank(ab)
                    wt("tensor", "sV", vtg)

                    def w3mm(t, hs=hs, r=r, W3h=W3h, W3l=W3l):
                        for ts_ in range(2):
                            for dh in range(2):
                                ps = pb[a_banks[ts_ * 2 + dh]][:, 0:512]
                                last = mm3(t, ps,
                                           w23_h2h[:, r, ts_ * 128:(ts_ + 1) * 128],
                                           w23_h2l[:, r, ts_ * 128:(ts_ + 1) * 128],
                                           W3h[:, hs, dh * 512:(dh + 1) * 512],
                                           W3l[:, hs, dh * 512:(dh + 1) * 512],
                                           hs == 0, hs == 3)
                        return last
                    grp("tensor", "sT", w3mm)
                wt("vector", "sT")

                def a_evac(v, c=c, it=it):
                    for ts_ in range(2):
                        tc = it * 2 + ts_
                        for dh in range(2):
                            ps = pb[a_banks[ts_ * 2 + dh]][:, 0:512]
                            dst = a_acc[:, tc, dh * 512:(dh + 1) * 512]
                            if c == 0:
                                last = v.scalar_tensor_tensor(
                                    dst, ps, 1.0, b3bc[:, dh * 512:(dh + 1) * 512],
                                    ALU.mult, ALU.add)
                            else:
                                last = v.tensor_tensor(dst, dst, ps, ALU.add)
                    return last
                vtg = grp("vector", "sV", a_evac)
                for ab in a_banks:
                    bank_free[ab] = ("sV", vtg)
                if next_stage:
                    next_stage.pop(0)()
                if round_items and (c * 8 + it) % 4 == 1:
                    round_items.pop(0)()

    def phase_ln1(seq):
        store_guard = 0
        for tc in range(16):
            tile = seq * 16 + tc
            buf = tc % 2
            wt("sync", "sV")
            tg = dma(ln_x[:, buf, :], d_xtok[seq, tc * 128:(tc + 1) * 128, :])
            wt("vector", "sD", tg)
            vz = grp("vector", "sV", lambda v, tc=tc, buf=buf: (
                v.tensor_tensor(ln_z[...], ln_x[:, buf, :], a_acc[:, tc, :],
                                ALU.add),
                v.reduce_sum(col_m[...], ln_z[...], axis=AX.X))[-1])
            wt("scalar", "sV", vz)
            ssq = grp("scalar", "sS", lambda sE:
                      sE.activation(ln_zsq[...], ln_z[...], AF.Square))
            wt("vector", "sS", ssq)
            vchain(
                lambda v: v.reduce_sum(col_v[...], ln_zsq[...], axis=AX.X),
                lambda v: v.tensor_scalar_mul(col_m[...], col_m[...], 1.0 / D),
                lambda v: v.tensor_scalar_mul(col_v[...], col_v[...], 1.0 / D),
                lambda v: v.tensor_tensor(col_t0[...], col_m[...], col_m[...], ALU.mult),
                lambda v: v.tensor_tensor(col_v[...], col_v[...], col_t0[...],
                                          ALU.subtract))
            col_rsqrt_newton(col_v, col_r, col_t1, EPS_LN)
            if store_guard:
                wt("vector", "sD", store_guard)

            grp("vector", "sV", lambda v, buf=buf: (
                v.tensor_scalar(ln_x1[:, buf, :], ln_z[...], col_m[...], None,
                                ALU.subtract),
                v.tensor_scalar(ln_x1[:, buf, :], ln_x1[:, buf, :], col_r[...],
                                None, ALU.mult))[-1])
            vx1 = vchain(
                lambda v, buf=buf: v.reduce_max(col_am[...], ln_x1[:, buf, :],
                                                axis=AX.X,
                                                apply_absolute_value=True),
                lambda v: v.tensor_scalar_max(col_am[...], col_am[...], EPS_Q),
                lambda v: v.reciprocal(col_t0[...], col_am[...]),
                lambda v, tile=tile: (
                    v.tensor_scalar_mul(col_sx[:, tile:tile + 1], col_t0[...], 127.0),
                    v.tensor_tensor(col_t1[...], col_am[...], md_g[...], ALU.mult))[-1],
                lambda v, tile=tile: (
                    v.tensor_scalar_mul(col_dqg[:, tile:tile + 1], col_t1[...],
                                        1.0 / 127.0),
                    v.tensor_tensor(col_t1[...], col_am[...], md_u[...], ALU.mult))[-1],
                lambda v, tile=tile: v.tensor_scalar_mul(
                    col_rsu[:, tile:tile + 1], col_t1[...], 1.0 / 127.0))
            wt("sync", "sV", vx1)
            dma(d_x1[tile * 128:(tile + 1) * 128, :], ln_x1[:, buf, :])
            wt("scalar", "sV", vx1)
            sc = grp("scalar", "sS", lambda sE, buf=buf, tile=tile: (
                sE.activation(ln_c32[...], ln_x1[:, buf, :], AF.Copy,
                              scale=col_sx[:, tile:tile + 1]),
                sE.copy(ln_cf[...], ln_c32[...]))[-1])
            wt("vector", "sS", sc)

            def q_ops(v, buf=buf, tile=tile):
                v.scalar_tensor_tensor(ln_d[...], ln_x1[:, buf, :],
                                       col_sx[:, tile:tile + 1], ln_cf[...],
                                       ALU.mult, ALU.subtract)
                v.tensor_single_scalar(ln_gt[...], ln_d[...], 0.5, ALU.is_gt)
                v.tensor_single_scalar(ln_d[...], ln_d[...], -0.5, ALU.is_lt)
                v.tensor_tensor(ln_gt[...], ln_gt[...], ln_d[...], ALU.subtract)
                return v.tensor_tensor(ln_xq[:, buf, :], ln_cf[...], ln_gt[...],
                                       ALU.add)
            vq = grp("vector", "sV", q_ops)
            wt("sync", "sV", vq)
            store_guard = dma(d_xq[tile * 128:(tile + 1) * 128, :], ln_xq[:, buf, :])

    wround_state = {"i": 0, "done": {}}

    def weight_round(d_wt, dst_view, thr_p, thr_n):
        ncol = d_wt.shape[1]
        nh = max(1, ncol // 2048)
        w_cols = ncol // nh
        done_tgt = wround_state["done"]
        for q in range(d_wt.shape[0] // 128):
            for h in range(nh):
                i = wround_state["i"]
                b = i % 2
                if i >= 2:
                    wt("sync", "sV", done_tgt[i - 2])
                tg = dma(wr_st[:, b, 0:w_cols],
                         d_wt[q * 128:(q + 1) * 128, h * w_cols:(h + 1) * w_cols])
                wt("vector", "sD", tg)
                done_tgt[i] = grp("vector", "sV", lambda v, q=q, h=h, b=b: (
                    v.tensor_scalar(wr_lt[:, 0:w_cols], wr_st[:, b, 0:w_cols],
                                    thr_n[...], None, ALU.is_lt),
                    v.scalar_tensor_tensor(
                        dst_view[:, q, h * w_cols:(h + 1) * w_cols],
                        wr_st[:, b, 0:w_cols], thr_p[...], wr_lt[:, 0:w_cols],
                        ALU.is_gt, ALU.subtract))[-1])
                wround_state["i"] = i + 1
        return cnt["sV"]

    def phase_ffna(w_ready_V):
        pbT = [pb[i][...].bitcast(BF16) for i in range(8)]
        NT = T // 128
        load_tgt = {}
        tpose_done = {}
        pq_store = {}

        def load_xq(i):
            if i >= NT:
                return
            b = i % 2
            if i >= 2:
                wt("sync", "sT", tpose_done.get(i - 2, 0))
            load_tgt[i] = dma(fa_xq[:, b, :], d_xq[i * 128:(i + 1) * 128, :])

        def tpose_xq(i):
            if i >= NT:
                return
            b = i % 2
            wt("tensor", "sD", load_tgt[i])
            for k in range(DC):
                bk_i = 4 + k % 4
                pe_wait_bank(bk_i)
                ptg = grp("tensor", "sT", lambda t, k=k, b=b, bk_i=bk_i:
                          t.transpose(pbT[bk_i][:, 0:128],
                                      fa_xq[:, b, k * 128:(k + 1) * 128],
                                      id16_sb[...]))
                wt("scalar", "sT", ptg)
                stg = grp("scalar", "sS", lambda sE, k=k, b=b, bk_i=bk_i:
                          sE.copy(fa_xqT[:, b, k, :], pbT[bk_i][:, 0:128]))
                bank_free[bk_i] = ("sS", stg)
            tpose_done[i] = cnt["sT"]
            return cnt["sS"]

        load_xq(0)
        xqT_rdy = {0: tpose_xq(0)}
        load_xq(1)
        for tile in range(NT):
            b = tile % 2
            wt("tensor", "sS", xqT_rdy[tile])
            if tile == 0:
                wt("tensor", "sV", w_ready_V)
            sil_guard = 0
            for hsl in range(8):
                bg, bu = (hsl % 2) * 2, (hsl % 2) * 2 + 1
                pe_wait_bank(bg)
                pe_wait_bank(bu)

                def gumm(t, b=b, hsl=hsl, bg=bg, bu=bu):
                    for k in range(DC):
                        t.matmul(pb[bg][:, 0:512], fa_xqT[:, b, k, :],
                                 wgq[:, k, hsl * 512:(hsl + 1) * 512],
                                 start=(k == 0), stop=(k == DC - 1))
                    for k in range(DC):
                        last = t.matmul(pb[bu][:, 0:512], fa_xqT[:, b, k, :],
                                        wuq[:, k, hsl * 512:(hsl + 1) * 512],
                                        start=(k == 0), stop=(k == DC - 1))
                    return last
                ptg = grp("tensor", "sT", gumm)
                if hsl == 3:
                    load_xq(tile + 2)
                    xqT_rdy[tile + 1] = tpose_xq(tile + 1)
                wt("scalar", "sT", ptg)
                if sil_guard:
                    wt("scalar", "sV", sil_guard)
                stg = grp("scalar", "sS", lambda sE, tile=tile, bg=bg:
                          sE.activation(fa_sil[...], pb[bg][:, 0:512], AF.Silu,
                                        scale=col_dqg[:, tile:tile + 1]))
                bank_free[bg] = ("sS", stg)
                wt("vector", "sS", stg)
                if tile > 0 and hsl == 0:
                    wt("vector", "sD", pq_store.get(tile - 1, 0))
                vtg = grp("vector", "sV", lambda v, hsl=hsl, bu=bu: (
                    v.tensor_tensor(fa_p[:, hsl * 512:(hsl + 1) * 512],
                                    pb[bu][:, 0:512], fa_sil[...], ALU.mult),
                    v.reduce_max(col_pp[:, hsl:hsl + 1],
                                 fa_p[:, hsl * 512:(hsl + 1) * 512], axis=AX.X,
                                 apply_absolute_value=True))[-1]
                          )
                sil_guard = vtg
                bank_free[bu] = ("sV", vtg)
            vchain(
                lambda v: v.reduce_max(col_am[...], col_pp[...], axis=AX.X),
                lambda v, tile=tile: v.tensor_tensor(
                    col_am[...], col_am[...], col_rsu[:, tile:tile + 1],
                    ALU.mult),
                lambda v: v.tensor_scalar_max(col_am[...], col_am[...], EPS_Q),
                lambda v: v.reciprocal(col_t0[...], col_am[...]),
                lambda v: (
                    v.tensor_scalar_mul(col_t0[...], col_t0[...], 127.0),
                    v.tensor_tensor(col_t1[...], col_am[...], md_d[...],
                                    ALU.mult))[-1],
                lambda v, tile=tile: (
                    v.tensor_scalar_mul(col_dqf[:, tile:tile + 1], col_t1[...],
                                        1.0 / 127.0)),
                lambda v, tile=tile: v.tensor_tensor(
                    col_t0[...], col_t0[...], col_rsu[:, tile:tile + 1],
                    ALU.mult))
            sp_ready = cnt["sV"]
            for sl in range(8):
                rr = sl % 2
                cf = fa_cf[:, rr * 512:(rr + 1) * 512]
                if sl == 0:
                    wt("scalar", "sV", sp_ready)
                else:
                    wt("scalar", "sV")
                sc = grp("scalar", "sS", lambda sE, sl=sl, cf=cf: (
                    sE.activation(fa_c32[...], fa_p[:, sl * 512:(sl + 1) * 512],
                                  AF.Copy, scale=col_t0[...]),
                    sE.copy(cf, fa_c32[...]))[-1])
                wt("vector", "sS", sc)

                def qops(v, sl=sl, cf=cf):
                    v.scalar_tensor_tensor(fa_d[...],
                                           fa_p[:, sl * 512:(sl + 1) * 512],
                                           col_t0[...], cf, ALU.mult,
                                           ALU.subtract)
                    v.scalar_tensor_tensor(fa_gt[...], fa_d[...], 0.5,
                                           cf, ALU.is_gt, ALU.add)
                    v.tensor_single_scalar(fa_d[...], fa_d[...], -0.5, ALU.is_lt)
                    return v.tensor_tensor(fa_pq[:, sl * 512:(sl + 1) * 512],
                                           fa_gt[...], fa_d[...], ALU.subtract)
                grp("vector", "sV", qops)
            wt("sync", "sV")
            pq_store[tile] = dma(d_pq[tile * 128:(tile + 1) * 128, :], fa_pq[...])

    def phase_ffnb(upc_ready):
        y_store = {}
        half_done = {}
        half_tg = {}
        tgx = {}

        def load_half(g, half):
            if g > 0:
                wt("sync", "sT", half_done[(g - 1, half)])
            tg = 0
            for c_ in range(HC):
                tg = dma(pqT_g[:, c_, half * 512:(half + 1) * 512],
                         d_pq[g * GB + half * 512: g * GB + (half + 1) * 512,
                              c_ * 128:(c_ + 1) * 128],
                         transpose=True)
            half_tg[(g, half)] = tg

        wt("sync", "sT")
        load_half(0, 0)
        wt("sync", "sV")
        for f in range(4):
            tgx[f] = dma(fb_x1[:, f % 4, :], d_x1[f * 128:(f + 1) * 128, :])
        load_half(0, 1)
        NG = T // GB
        for g in range(NG):
            for f in range(GB // 128):
                tile = g * (GB // 128) + f
                xbuf = f % 4
                buf = f % 2
                wt("tensor", "sD", half_tg[(g, 0 if f < 4 else 1)])
                if g == 0 and f == 0:
                    wt("tensor", "sV", upc_ready)
                for dh in range(2):
                    pe_wait_bank(dh)

                    def dmm(t, f=f, dh=dh):
                        for hc in range(HC):
                            last = t.matmul(pb[dh][:, 0:512],
                                            pqT_g[:, hc, f * 128:(f + 1) * 128],
                                            wdq[:, hc, dh * 512:(dh + 1) * 512],
                                            start=(hc == 0), stop=(hc == HC - 1))
                        return last
                    grp("tensor", "sT", dmm)
                ptg = cnt["sT"]
                if f == 3:
                    half_done[(g, 0)] = ptg
                elif f == 7:
                    half_done[(g, 1)] = ptg
                if f == 4 and g + 1 < NG:
                    load_half(g + 1, 0)
                elif f == 7 and g + 1 < NG:
                    load_half(g + 1, 1)
                wt("vector", "sT", ptg)
                wt("vector", "sD", tgx[tile])

                def z2ops(v, tile=tile, buf=xbuf):
                    for dh in range(2):
                        v.scalar_tensor_tensor(fb_z[:, dh * 512:(dh + 1) * 512],
                                               pb[dh][:, 0:512],
                                               col_dqf[:, tile:tile + 1],
                                               fb_x1[:, buf, dh * 512:(dh + 1) * 512],
                                               ALU.mult, ALU.add)
                    return v.reduce_sum(col_m[...], fb_z[...], axis=AX.X)
                vz = grp("vector", "sV", z2ops)
                bank_free[0] = ("sV", vz)
                bank_free[1] = ("sV", vz)
                if tile + 4 < T // 128:
                    wt("sync", "sV", vz)
                    tgx[tile + 4] = dma(fb_x1[:, xbuf, :],
                                        d_x1[(tile + 4) * 128:(tile + 5) * 128, :])
                wt("scalar", "sV", vz)
                ssq = grp("scalar", "sS", lambda sE:
                          sE.activation(fb_zsq[...], fb_z[...], AF.Square))
                wt("vector", "sS", ssq)
                vchain(
                    lambda v: v.reduce_sum(col_v[...], fb_zsq[...], axis=AX.X),
                    lambda v: v.tensor_scalar_mul(col_m[...], col_m[...], 1.0 / D),
                    lambda v: v.tensor_scalar_mul(col_v[...], col_v[...], 1.0 / D),
                    lambda v: v.tensor_tensor(col_t0[...], col_m[...], col_m[...],
                                              ALU.mult),
                    lambda v: v.tensor_tensor(col_v[...], col_v[...], col_t0[...],
                                              ALU.subtract))
                col_rsqrt_newton(col_v, col_r, col_t1, EPS_LN)
                if tile >= 2:
                    wt("vector", "sD", y_store.get(tile - 2, 0))
                vy = grp("vector", "sV", lambda v, buf=buf: (
                    v.tensor_scalar(fb_y[:, buf, :], fb_z[...], col_m[...], None,
                                    ALU.subtract),
                    v.tensor_scalar(fb_y[:, buf, :], fb_y[:, buf, :], col_r[...],
                                    None, ALU.mult))[-1])
                wt("sync", "sV", vy)
                y_store[tile] = dma(d_y[tile * 128:(tile + 1) * 128, :],
                                    fb_y[:, buf, :])

    for seq in range(NSEQ):
        sync_barrier()
        wr = stream_weight_Wc(d_w["wkT"], Wc_hi, Wc_lo)
        phase_kbuild(seq, wr)
        sync_barrier()
        wr = stream_weight_Wc(d_w["wvT"], Wc_hi, Wc_lo)
        phase_vbuild(seq, wr)
        sync_barrier()
        wr = stream_weight_Wc(d_w["wqT"], Wc_hi, Wc_lo)
        phase_attn(seq, wr)
    sync_barrier()
    wr = stream_weight_Wc(d_w["w1T"], Wc_hi, Wc_lo)
    phase_w1(wr)
    sync_barrier()
    weight_mean_finalize()
    round_items = round_chunk_emitters()
    for seq in range(NSEQ):
        sync_barrier()
        phase_w23(seq, round_items)
        sync_barrier()
        phase_ln1(seq)
    sync_barrier()
    for em_ in round_items:
        em_()
    weight_round(d_wgT, wgq, thr_g, thr_gn)
    weight_round(d_wuT, wuq, thr_u, thr_un)
    gu_ready = cnt["sV"]
    sync_barrier()
    phase_ffna(gu_ready)
    sync_barrier()
    wt("scalar", "sT")
    wt("scalar", "sV")
    grp("scalar", "sS", lambda sE:
        sE.copy(RA[...], RCI8[:, 0:32768]))
    upc_s = cnt["sS"]
    wt("vector", "sS", upc_s)
    upc_v = grp("vector", "sV", lambda v: v.tensor_copy(col_t0[...], col_t0[...]))
    phase_ffnb(upc_v)
    em("sync", lambda e, h=sem["sD"], v=cnt["sD"]: e.wait_ge(h, v))

    with nc.Block() as block:
        @block.tensor
        def _(t):
            for fn in plan["tensor"]:
                fn(t)

        @block.vector
        def _(v):
            for fn in plan["vector"]:
                fn(v)

        @block.scalar
        def _(sE):
            for fn in plan["scalar"]:
                fn(sE)

        @block.gpsimd
        def _(g_):
            for fn in plan["gpsimd"]:
                fn(g_)

        @block.sync
        def _(sy):
            for fn in plan["sync"]:
                fn(sy)
    ctx.close()
    return nc


_NC_CACHE = []


def kernel(L0_attn_x, L0_attn_Wq, L0_attn_bq, L0_attn_Wk, L0_attn_bk,
           L0_attn_Wv, L0_attn_bv, L0_attn_W1, L0_attn_b1, L0_attn_W2,
           L0_attn_b2, L0_attn_W3, L0_attn_b3, L0_ffn_w_gate, L0_ffn_w_up,
           L0_ffn_w_down):
    f32 = lambda a: np.ascontiguousarray(np.asarray(a, dtype=np.float32))
    x = f32(L0_attn_x)
    wT = {
        "wqT": f32(L0_attn_Wq).T, "wkT": f32(L0_attn_Wk).T,
        "wvT": f32(L0_attn_Wv).T, "w1T": f32(L0_attn_W1).T,
        "w2T": f32(L0_attn_W2).T, "w3T": f32(L0_attn_W3).T,
        "wgT": f32(L0_ffn_w_gate).T, "wuT": f32(L0_ffn_w_up).T,
        "wdT": f32(L0_ffn_w_down).T,
    }
    wT = {k: np.ascontiguousarray(v) for k, v in wT.items()}
    bcol = lambda b, nchunk: np.ascontiguousarray(
        f32(b).reshape(nchunk, 128).T)
    shared = {
        **wT,
        "bq": bcol(L0_attn_bq, DC), "bk": bcol(L0_attn_bk, DC),
        "b1": bcol(L0_attn_b1, DC), "b2": bcol(L0_attn_b2, HC),
        "bvrow": f32(L0_attn_bv).reshape(1, D),
        "b3row": f32(L0_attn_b3).reshape(1, D),
        "ident16": np.eye(128, dtype=ml_dtypes.bfloat16),
    }
    if not _NC_CACHE:
        _NC_CACHE.append(build())
    nc = _NC_CACHE[0]
    in_maps = []
    for c in range(NCORES):
        xs = x[NSEQ * c: NSEQ * (c + 1)]
        in_maps.append({
            "xT": np.ascontiguousarray(xs.transpose(0, 2, 1)),
            "xtok": np.ascontiguousarray(xs),
            **shared,
        })
    res = run_bass_kernel_spmd(nc, in_maps, core_ids=list(range(NCORES)))
    out = np.concatenate([np.asarray(r["y"]).reshape(NSEQ, S, D)
                          for r in res.results], axis=0)
    return out


# version 17
# speedup vs baseline: 1.0904x; 1.0669x over previous
import numpy as np
import ml_dtypes
from contextlib import ExitStack
import concourse.bass as bass
import concourse.mybir as mybir
from concourse.bass_utils import run_bass_kernel_spmd

F32 = mybir.dt.float32
BF16 = mybir.dt.bfloat16
I32 = mybir.dt.int32
I8 = mybir.dt.int8
AF = mybir.ActivationFunctionType
ALU = mybir.AluOpType
AX = mybir.AxisListType

NCORES = 8
S, D, HF = 2048, 1024, 4096
NSEQ = 2
T = NSEQ * S
DC, HC = D // 128, HF // 128
TB = 256
TQ = 256
TW = 256
GA = 512
GB = 1024
EPS_LN = 1e-5
EPS_NORM = 1e-12
EPS_Q = 1e-5


def build():
    nc = bass.Bass("TRN2", target_bir_lowering=False, detect_race_conditions=False)

    d_xT = nc.dram_tensor("xT", [NSEQ, D, S], F32, kind="ExternalInput")
    d_xtok = nc.dram_tensor("xtok", [NSEQ, S, D], F32, kind="ExternalInput")
    d_w = {n: nc.dram_tensor(n, [D, D], F32, kind="ExternalInput")
           for n in ["wqT", "wkT", "wvT", "w1T"]}
    d_w2T = nc.dram_tensor("w2T", [D, HF], F32, kind="ExternalInput")
    d_w3T = nc.dram_tensor("w3T", [HF, D], F32, kind="ExternalInput")
    d_wgT = nc.dram_tensor("wgT", [D, HF], F32, kind="ExternalInput")
    d_wuT = nc.dram_tensor("wuT", [D, HF], F32, kind="ExternalInput")
    d_wdT = nc.dram_tensor("wdT", [HF, D], F32, kind="ExternalInput")
    d_bq = nc.dram_tensor("bq", [128, DC], F32, kind="ExternalInput")
    d_bk = nc.dram_tensor("bk", [128, DC], F32, kind="ExternalInput")
    d_b1 = nc.dram_tensor("b1", [128, DC], F32, kind="ExternalInput")
    d_b2 = nc.dram_tensor("b2", [128, HC], F32, kind="ExternalInput")
    d_bv = nc.dram_tensor("bvrow", [1, D], F32, kind="ExternalInput")
    d_b3 = nc.dram_tensor("b3row", [1, D], F32, kind="ExternalInput")
    d_id16 = nc.dram_tensor("ident16", [128, 128], BF16, kind="ExternalInput")
    d_y = nc.dram_tensor("y", [T, D], F32, kind="ExternalOutput")

    d_outf = nc.dram_tensor("scr_outf", [D, T], F32)
    d_h1h = nc.dram_tensor("scr_h1h", [D, T], BF16)
    d_h1l = nc.dram_tensor("scr_h1l", [D, T], BF16)
    d_x1 = nc.dram_tensor("scr_x1", [T, D], F32)
    d_xq = nc.dram_tensor("scr_xq", [T, D], BF16)
    d_pq = nc.dram_tensor("scr_pq", [T, HF], BF16)

    RA = nc.alloc_sbuf_tensor("RA", [128, 32768], BF16)
    RB = nc.alloc_sbuf_tensor("RB", [128, 32768], BF16)
    RC = nc.alloc_sbuf_tensor("RC", [128, 16384], BF16)
    RK = nc.alloc_sbuf_tensor("RK", [128, 22528], BF16)
    RA32, RB32, RK32 = RA.bitcast(F32), RB.bitcast(F32), RK.bitcast(F32)
    RKI32 = RK.bitcast(I32)
    RCI8 = RC.bitcast(I8)
    col_pp = None

    KnT_hi = RA[:, 0:16384].rearrange("p (c t) -> p c t", c=DC)
    KnT_lo = RA[:, 16384:32768].rearrange("p (c t) -> p c t", c=DC)
    V_hi = RB[:, 0:16384].rearrange("p (c d) -> p c d", c=16)
    V_lo = RB[:, 16384:32768].rearrange("p (c d) -> p c d", c=16)
    Wc_hi = RC[:, 0:8192].rearrange("p (c d) -> p c d", c=DC)
    Wc_lo = RC[:, 8192:16384].rearrange("p (c d) -> p c d", c=DC)
    def w23_views(par):
        base = par * 16384
        mk2 = lambda off: RA[:, base + off: base + off + 4096].rearrange(
            "p (c d) -> p c d", c=DC)
        mk3 = lambda off: RA[:, base + off: base + off + 4096].rearrange(
            "p (c d) -> p c d", c=4)
        return mk2(0), mk2(4096), mk3(8192), mk3(12288)
    RC32 = RC.bitcast(F32)
    wrc_st = RC32[:, 0:4096].rearrange("p (b t) -> p b t", b=2)
    wrc_gt = RC32[:, 4096:6144]
    wrc_lt = RC32[:, 6144:8192]
    a_acc = RB32.rearrange("p (c d) -> p c d", c=16)
    wgq = RA.rearrange("p (c h) -> p c h", c=DC)
    wuq = RB.rearrange("p (c h) -> p c h", c=DC)
    wdq8 = RCI8.rearrange("p (c d) -> p c d", c=HC)
    wdq = RA.rearrange("p (c d) -> p c d", c=HC)
    pqT_g = RB.rearrange("p (c t) -> p c t", c=HC)

    def rk(off, shape, dt=F32):
        n = int(np.prod(shape[1:]))
        sz = 2 if dt == BF16 else 4
        assert off % 4 == 0 and off + n * sz <= 45056, (off, n, sz)
        if dt == F32:
            base = RK32[:, off // 4: off // 4 + n]
        elif dt == BF16:
            base = RK[:, off // 2: off // 2 + n]
        elif dt == I32:
            base = RKI32[:, off // 4: off // 4 + n]
        if len(shape) == 3:
            return base.rearrange("p (a b) -> p a b", a=shape[1])
        if len(shape) == 4:
            return base.rearrange("p (a b c) -> p a b c", a=shape[1], b=shape[2])
        return base

    xt_f = rk(0, [128, DC, 256], F32)
    xt_hi = rk(8192, [128, DC, 256], BF16)
    xt_lo = rk(12288, [128, DC, 256], BF16)
    qf = rk(16384, [128, DC, 256], F32)
    vb_scr = rk(16384, [128, 512], F32)
    qsq = rk(24576, [128, DC, 256], F32)
    qn_hi = rk(32768, [128, DC, 256], BF16)
    qn_lo = rk(36864, [128, DC, 256], BF16)
    e_f = rk(16384, [128, 256], F32)
    e_hi = rk(24576, [128, 16, 256], BF16)
    e_lo = rk(0, [128, 16, 256], BF16)
    outf_ev = rk(16384, [128, DC, 256], F32)
    w1_h1f = rk(24576, [128, 2, 256], F32)
    bvbc = rk(32768, [128, D], F32)
    rowd = rk(36864, [1, D], F32)
    w23_h1h = rk(0, [128, 2, DC, TW], BF16)
    w23_h1l = rk(8192, [128, 2, DC, TW], BF16)
    w23_h2f = rk(16384, [128, 2, TW], F32)
    w23_h2h = rk(18432, [128, 2, TW], BF16)
    w23_h2l = rk(19456, [128, 2, TW], BF16)
    w23_wst = rk(20480, [128, DC, 256], F32)
    mean_st = rk(20480, [128, 4096], F32)
    wd_st = rk(28672, [128, 2, 1024], F32)
    wd_lt = rk(36864, [128, 1024], F32)
    b3bc = rk(40960, [128, D], F32)
    rowd_w = rk(36864, [1, D], F32)
    ln_x = rk(0, [128, 2, D], F32)
    ln_z = rk(8192, [128, D], F32)
    ln_zsq = rk(12288, [128, D], F32)
    ln_x1 = rk(16384, [128, 2, D], F32)
    ln_c32 = rk(24576, [128, D], I32)
    ln_cf = rk(28672, [128, D], F32)
    ln_d = rk(32768, [128, D], F32)
    ln_gt = rk(36864, [128, D], F32)
    ln_xq = rk(40960, [128, 2, D], BF16)
    wr_st = rk(0, [128, 2, 2048], F32)
    wr_gt = rk(16384, [128, 2048], F32)
    wr_lt = rk(24576, [128, 2048], F32)
    fa_xq = rk(0, [128, 2, D], BF16)
    fa_xqT = RK[:, 2048:4096].rearrange("p (b c t) -> p b c t", b=2, c=DC)
    fa_p = rk(8192, [128, HF], F32)
    fa_sil = rk(24576, [128, 512], F32)
    fa_pq = rk(26624, [128, HF], BF16)
    fa_c32 = rk(34816, [128, 512], I32)
    fa_cf = rk(36864, [128, 1024], F32)
    fa_d = rk(40960, [128, 512], F32)
    fa_gt = rk(43008, [128, 512], F32)
    fb_x1 = rk(0, [128, 4, D], F32)
    fb_z = rk(16384, [128, D], F32)
    fb_zsq = rk(20480, [128, D], F32)
    fb_y = rk(24576, [128, 2, D], F32)

    row_a = rk(40960, [1, 256], F32)
    row_b = rk(41984, [1, 256], F32)
    row_c = rk(43008, [1, 256], F32)
    row_bc = rk(44032, [128, 256], F32)
    ones_col32 = nc.alloc_sbuf_tensor("ones_col32", [128, 1], F32)
    ones_col16 = nc.alloc_sbuf_tensor("ones_col16", [128, 1], BF16)
    ones_row32 = nc.alloc_sbuf_tensor("ones_row32", [1, 128], F32)
    id16_sb = nc.alloc_sbuf_tensor("id16_sb", [128, 128], BF16)
    bq_sb = nc.alloc_sbuf_tensor("bq_sb", [128, DC], F32)
    bk_sb = nc.alloc_sbuf_tensor("bk_sb", [128, DC], F32)
    b1_sb = nc.alloc_sbuf_tensor("b1_sb", [128, DC], F32)
    b2_sb = nc.alloc_sbuf_tensor("b2_sb", [128, HC], F32)
    col_sx = nc.alloc_sbuf_tensor("col_sx", [128, 32], F32)
    col_dqg = nc.alloc_sbuf_tensor("col_dqg", [128, 32], F32)
    col_rsu = nc.alloc_sbuf_tensor("col_rsu", [128, 32], F32)
    col_dqf = nc.alloc_sbuf_tensor("col_dqf", [128, 32], F32)
    col_am = nc.alloc_sbuf_tensor("col_am", [128, 1], F32)
    col_t0 = nc.alloc_sbuf_tensor("col_t0", [128, 1], F32)
    col_t1 = nc.alloc_sbuf_tensor("col_t1", [128, 1], F32)
    col_m = nc.alloc_sbuf_tensor("col_m", [128, 1], F32)
    col_v = nc.alloc_sbuf_tensor("col_v", [128, 1], F32)
    col_r = nc.alloc_sbuf_tensor("col_r", [128, 1], F32)
    col_pp = nc.alloc_sbuf_tensor("col_pp", [128, 8], F32)
    wsum = nc.alloc_sbuf_tensor("wsum", [128, 3], F32)
    thr_g = nc.alloc_sbuf_tensor("thr_g", [128, 1], F32)
    thr_gn = nc.alloc_sbuf_tensor("thr_gn", [128, 1], F32)
    thr_u = nc.alloc_sbuf_tensor("thr_u", [128, 1], F32)
    thr_un = nc.alloc_sbuf_tensor("thr_un", [128, 1], F32)
    thr_d = nc.alloc_sbuf_tensor("thr_d", [128, 1], F32)
    thr_dn = nc.alloc_sbuf_tensor("thr_dn", [128, 1], F32)
    md_g = nc.alloc_sbuf_tensor("md_g", [128, 1], F32)
    md_u = nc.alloc_sbuf_tensor("md_u", [128, 1], F32)
    md_d = nc.alloc_sbuf_tensor("md_d", [128, 1], F32)

    pb = [nc.alloc_psum_tensor(f"pb{i}", [128, 512], F32) for i in range(8)]
    PB_ROWS, PB_BC = 6, 7

    ctx = ExitStack()
    sem = {n: ctx.enter_context(nc.semaphore(n)) for n in
           ["sD", "sT", "sV", "sS", "sG"]}
    cnt = {n: 0 for n in sem}
    plan = {e: [] for e in ["tensor", "vector", "scalar", "sync", "gpsimd"]}

    def em(eng, fn):
        plan[eng].append(fn)

    def wt(eng, s, val=None):
        v = cnt[s] if val is None else val
        if v > 0:
            em(eng, lambda e, h=sem[s], v=v: e.wait_ge(h, v))
        return v

    def dma(out_ap, in_ap, transpose=False):
        cnt["sD"] += 16
        if transpose:
            em("sync", lambda e, o=out_ap, i=in_ap, h=sem["sD"]:
               e.dma_start_transpose(o, i).then_inc(h, 16))
        else:
            em("sync", lambda e, o=out_ap, i=in_ap, h=sem["sD"]:
               e.dma_start(o, i).then_inc(h, 16))
        return cnt["sD"]

    def grp(eng, s, emitter):
        cnt[s] += 1
        em(eng, lambda e, f=emitter, h=sem[s]: f(e).then_inc(h, 1))
        return cnt[s]

    bank_free = {i: ("sV", 0) for i in range(8)}

    def pe_wait_bank(i):
        s, v = bank_free[i]
        wt("tensor", s, v)

    def mm3(t, ps, lh, ll, rh, rl, first, last):
        t.matmul(ps, lh, rh, start=first, stop=False)
        t.matmul(ps, lh, rl, start=False, stop=False)
        return t.matmul(ps, ll, rh, start=False, stop=last)

    def split(v, src, hi, lo):
        v.tensor_copy(hi, src)
        return v.tensor_tensor(lo, src, hi, ALU.subtract)

    def rsqrt_newton_row(n, src_row, dst_row, tmp_row, guard=None):
        s1 = grp("scalar", "sS", lambda sE:
                 sE.activation(tmp_row[0:1, 0:n], src_row[0:1, 0:n], AF.Ln))
        wt("scalar", "sS", s1)
        stg = grp("scalar", "sS", lambda sE:
                  sE.activation(dst_row[0:1, 0:n], tmp_row[0:1, 0:n], AF.Exp,
                                scale=-0.5))
        wt("vector", "sS", stg)
        a, b_, s_ = dst_row[0:1, 0:n], tmp_row[0:1, 0:n], src_row[0:1, 0:n]
        ops = [
            lambda v: v.tensor_tensor(b_, a, a, ALU.mult),
            lambda v: v.tensor_tensor(b_, b_, s_, ALU.mult),
            lambda v: v.tensor_scalar(b_, b_, -0.5, 1.5, ALU.mult, ALU.add),
            lambda v: v.tensor_tensor(a, a, b_, ALU.mult),
        ]
        if guard is not None:
            ops.append(lambda v: v.tensor_scalar_min(a, a, guard))
        return vchain(*ops)

    def col_rsqrt_newton(src_col, dst_col, tmp_col, eps_bias):
        wt("vector", "sV")
        ve = grp("vector", "sV", lambda v:
                 v.tensor_scalar_add(src_col[...], src_col[...], eps_bias))
        wt("scalar", "sV", ve)
        s1 = grp("scalar", "sS", lambda sE:
                 sE.activation(tmp_col[...], src_col[...], AF.Ln))
        wt("scalar", "sS", s1)
        stg = grp("scalar", "sS", lambda sE:
                  sE.activation(dst_col[...], tmp_col[...], AF.Exp, scale=-0.5))
        wt("vector", "sS", stg)
        return vchain(
            lambda v: v.tensor_tensor(tmp_col[...], dst_col[...], dst_col[...], ALU.mult),
            lambda v: v.tensor_tensor(tmp_col[...], tmp_col[...], src_col[...], ALU.mult),
            lambda v: v.tensor_scalar(tmp_col[...], tmp_col[...], -0.5, 1.5, ALU.mult, ALU.add),
            lambda v: v.tensor_tensor(dst_col[...], dst_col[...], tmp_col[...], ALU.mult))

    def vchain(*emitters):
        last = 0
        for i, f in enumerate(emitters):
            wt("vector", "sV", last if i > 0 else None)
            last = grp("vector", "sV", f)
        return last

    def bcast_row(row_ap, n, dst_sbuf):
        pe_wait_bank(PB_BC)
        wt("tensor", "sV")
        ptg = grp("tensor", "sT", lambda t:
                  t.matmul(pb[PB_BC][:, 0:n], ones_row32[...], row_ap[0:1, 0:n],
                           start=True, stop=True))
        wt("vector", "sT", ptg)
        vtg = grp("vector", "sV", lambda v:
                  v.tensor_copy(dst_sbuf[:, 0:n], pb[PB_BC][:, 0:n]))
        bank_free[PB_BC] = ("sV", vtg)
        return vtg

    dma(bq_sb[...], d_bq[...])
    dma(id16_sb[...], d_id16[...])
    dma(bk_sb[...], d_bk[...])
    dma(b1_sb[...], d_b1[...])
    last_d = dma(b2_sb[...], d_b2[...])
    wt("vector", "sD", last_d)
    grp("vector", "sV", lambda v: (
        v.memset(ones_col32[...], 1.0),
        v.memset(ones_col16[...], 1.0),
        v.memset(ones_row32[...], 1.0))[-1])

    def build_bias_bcast(d_row, row_stage, dst_bc):
        wt("sync", "sV")
        tg = dma(row_stage[0:1, 0:D], d_row[...])
        for h in range(2):
            pe_wait_bank(PB_BC)
            wt("tensor", "sD", tg)
            wt("tensor", "sV")
            ptg = grp("tensor", "sT", lambda t, h=h:
                      t.matmul(pb[PB_BC][:, 0:512], ones_row32[...],
                               row_stage[0:1, h * 512:(h + 1) * 512],
                               start=True, stop=True))
            wt("vector", "sT", ptg)
            vtg = grp("vector", "sV", lambda v, h=h:
                      v.tensor_copy(dst_bc[:, h * 512:(h + 1) * 512],
                                    pb[PB_BC][:, 0:512]))
            bank_free[PB_BC] = ("sV", vtg)
        return cnt["sV"]

    def sync_barrier():
        wt("sync", "sT")
        wt("sync", "sV")
        wt("sync", "sS")
        wt("sync", "sD")

    def stream_weight_generic(d_wt, hi_view, lo_view, stage2, ncols, wslice=128):
        split_tgt = {}
        for q in range(ncols // wslice):
            c0 = q * wslice
            b = q % 2
            st = stage2[:, :, b * wslice:(b + 1) * wslice]
            if q >= 2:
                wt("sync", "sV", split_tgt[q - 2])
            tg = dma(st, d_wt[:, c0:c0 + wslice]
                     .rearrange("(c p) t -> p c t", p=128))
            wt("vector", "sD", tg)
            split_tgt[q] = grp("vector", "sV", lambda v, c0=c0, st=st: (
                v.tensor_copy(hi_view[:, :, c0:c0 + wslice], st),
                v.tensor_tensor(lo_view[:, :, c0:c0 + wslice], st,
                                hi_view[:, :, c0:c0 + wslice], ALU.subtract))[-1])
        return cnt["sV"]

    def stream_weight_Wc(d_wt, hi_view, lo_view):
        return stream_weight_generic(d_wt, hi_view, lo_view, xt_f, D)

    def proj_norm_tile(seq, t0, is_k, w_ready_V):
        width = TB if is_k else TQ
        wt("sync", "sV")
        tg = dma(xt_f[:, :, 0:width],
                 d_xT[seq, :, t0:t0 + width].rearrange("(c p) t -> p c t", p=128))
        wt("vector", "sD", tg)
        wt("vector", "sV", w_ready_V)
        vsp = grp("vector", "sV",
                  lambda v: split(v, xt_f[:, :, 0:width], xt_hi[:, :, 0:width],
                                  xt_lo[:, :, 0:width]))
        bias_sb = bk_sb if is_k else bq_sb
        for c in range(DC):
            bk_i = c % 2
            pe_wait_bank(bk_i)
            if c == 0:
                wt("tensor", "sV", vsp)

            def mms(t, c=c, bk_i=bk_i):
                for k in range(DC):
                    last = mm3(t, pb[bk_i][:, 0:width],
                               Wc_hi[:, k, c * 128:(c + 1) * 128],
                               Wc_lo[:, k, c * 128:(c + 1) * 128],
                               xt_hi[:, k, 0:width], xt_lo[:, k, 0:width],
                               k == 0, k == DC - 1)
                return last
            ptg = grp("tensor", "sT", mms)
            wt("scalar", "sT", ptg)
            wt("scalar", "sV")
            stg = grp("scalar", "sS", lambda sE, c=c, bk_i=bk_i:
                      sE.activation(qf[:, c, 0:width], pb[bk_i][:, 0:width],
                                    AF.Identity, bias=bias_sb[:, c:c + 1]))
            bank_free[bk_i] = ("sS", stg)
        wt("vector", "sS")
        vq = grp("vector", "sV", lambda v:
                 v.tensor_tensor(qsq[:, :, 0:width], qf[:, :, 0:width],
                                 qf[:, :, 0:width], ALU.mult))
        pe_wait_bank(PB_ROWS)
        wt("tensor", "sV", vq)

        def ssq_mm(t):
            for k in range(DC):
                last = t.matmul(pb[PB_ROWS][0:1, 0:width], ones_col32[...],
                                qsq[:, k, 0:width], start=(k == 0),
                                stop=(k == DC - 1))
            return last
        ptg = grp("tensor", "sT", ssq_mm)
        wt("scalar", "sT", ptg)
        wt("scalar", "sV")
        stg = grp("scalar", "sS", lambda sE:
                  sE.copy(row_a[0:1, 0:width], pb[PB_ROWS][0:1, 0:width]))
        bank_free[PB_ROWS] = ("sS", stg)
        wt("vector", "sS", stg)
        rsqrt_newton_row(width, row_a, row_b, row_c, guard=1.0 / EPS_NORM)
        bcast_row(row_b, width, row_bc)

        def norm_split(v):
            for c in range(DC):
                v.tensor_tensor(qf[:, c, 0:width], qf[:, c, 0:width],
                                row_bc[:, 0:width], ALU.mult)
            for c in range(DC):
                if is_k:
                    hi, lo = KnT_hi[:, c, t0:t0 + width], KnT_lo[:, c, t0:t0 + width]
                else:
                    hi, lo = qn_hi[:, c, 0:width], qn_lo[:, c, 0:width]
                v.tensor_copy(hi, qf[:, c, 0:width])
                last = v.tensor_tensor(lo, qf[:, c, 0:width], hi, ALU.subtract)
            return last
        return grp("vector", "sV", norm_split)

    def phase_kbuild(seq, w_ready):
        for it in range(S // TB):
            proj_norm_tile(seq, it * TB, True, w_ready)

    def phase_vbuild(seq, w_ready):
        build_bias_bcast(d_bv, rowd, bvbc)
        for it in range(S // TB):
            t0 = it * TB
            wt("sync", "sV")
            tg = dma(xt_f[...],
                     d_xT[seq, :, t0:t0 + TB].rearrange("(c p) t -> p c t", p=128))
            wt("vector", "sD", tg)
            wt("vector", "sV", w_ready)
            vsp = grp("vector", "sV",
                      lambda v: split(v, xt_f[...], xt_hi[...], xt_lo[...]))
            first = True
            for b in range(TB // 128):
                kc = it * (TB // 128) + b
                for dh in range(2):
                    bk_i = (b * 2 + dh) % 2
                    pe_wait_bank(bk_i)
                    if first:
                        wt("tensor", "sV", vsp)
                        first = False

                    def mms(t, b=b, dh=dh, bk_i=bk_i):
                        for k in range(DC):
                            last = mm3(t, pb[bk_i][:, 0:512],
                                       xt_hi[:, k, b * 128:(b + 1) * 128],
                                       xt_lo[:, k, b * 128:(b + 1) * 128],
                                       Wc_hi[:, k, dh * 512:(dh + 1) * 512],
                                       Wc_lo[:, k, dh * 512:(dh + 1) * 512],
                                       k == 0, k == DC - 1)
                        return last
                    ptg = grp("tensor", "sT", mms)
                    wt("vector", "sT", ptg)

                    def evac(v, kc=kc, dh=dh, bk_i=bk_i):
                        v.scalar_tensor_tensor(vb_scr[...], pb[bk_i][:, 0:512], 1.0,
                                               bvbc[:, dh * 512:(dh + 1) * 512],
                                               ALU.mult, ALU.add)
                        v.tensor_copy(V_hi[:, kc, dh * 512:(dh + 1) * 512], vb_scr[...])
                        return v.tensor_tensor(V_lo[:, kc, dh * 512:(dh + 1) * 512],
                                               vb_scr[...],
                                               V_hi[:, kc, dh * 512:(dh + 1) * 512],
                                               ALU.subtract)
                    vtg = grp("vector", "sV", evac)
                    bank_free[bk_i] = ("sV", vtg)

    def phase_attn(seq, w_ready):
        for qt in range(S // TQ):
            t0 = qt * TQ
            vn = proj_norm_tile(seq, t0, False, w_ready)
            pe_wait_bank(PB_ROWS)
            wt("tensor", "sV", vn)
            prev_lo = 0
            for kc in range(16):
                sb_i = kc % 2
                pe_wait_bank(sb_i)

                def smm(t, kc=kc, sb_i=sb_i):
                    for k in range(DC):
                        last = mm3(t, pb[sb_i][:, 0:TQ],
                                   KnT_hi[:, k, kc * 128:(kc + 1) * 128],
                                   KnT_lo[:, k, kc * 128:(kc + 1) * 128],
                                   qn_hi[:, k, 0:TQ], qn_lo[:, k, 0:TQ],
                                   k == 0, k == DC - 1)
                    return last
                ptg = grp("tensor", "sT", smm)
                wt("scalar", "sT", ptg)
                if prev_lo:
                    wt("scalar", "sV", prev_lo)
                stg = grp("scalar", "sS", lambda sE, kc=kc, sb_i=sb_i: (
                    sE.activation(e_f[...], pb[sb_i][:, 0:TQ], AF.Exp),
                    sE.activation(e_hi[:, kc, :], pb[sb_i][:, 0:TQ], AF.Exp))[-1])
                bank_free[sb_i] = ("sS", stg)
                wt("vector", "sS", stg)
                prev_lo = grp("vector", "sV", lambda v, kc=kc:
                              v.tensor_tensor(e_lo[:, kc, :], e_f[...],
                                              e_hi[:, kc, :], ALU.subtract))
                wt("tensor", "sV", prev_lo)

                def denom(t, kc=kc):
                    t.matmul(pb[PB_ROWS][0:1, 0:TQ], ones_col16[...],
                             e_hi[:, kc, :], start=(kc == 0), stop=False)
                    return t.matmul(pb[PB_ROWS][0:1, 0:TQ], ones_col16[...],
                                    e_lo[:, kc, :], start=False, stop=(kc == 15))
                grp("tensor", "sT", denom)
            wt("vector", "sT")
            wt("vector", "sV")
            vtg = vchain(
                lambda v: v.tensor_copy(row_a[0:1, 0:TQ], pb[PB_ROWS][0:1, 0:TQ]),
                lambda v: v.reciprocal(row_b[0:1, 0:TQ], row_a[0:1, 0:TQ]))
            bank_free[PB_ROWS] = ("sV", vtg)
            bcast_row(row_b, TQ, row_bc)
            rbc_v = cnt["sV"]
            for ds in range(DC):
                bk_i = 2 + (ds % 2)
                pe_wait_bank(bk_i)

                def avmm(t, ds=ds, bk_i=bk_i):
                    for kc in range(16):
                        t.matmul(pb[bk_i][:, 0:TQ],
                                 V_hi[:, kc, ds * 128:(ds + 1) * 128],
                                 e_hi[:, kc, :], start=(kc == 0), stop=False)
                        t.matmul(pb[bk_i][:, 0:TQ],
                                 V_hi[:, kc, ds * 128:(ds + 1) * 128],
                                 e_lo[:, kc, :], start=False, stop=False)
                        last = t.matmul(pb[bk_i][:, 0:TQ],
                                        V_lo[:, kc, ds * 128:(ds + 1) * 128],
                                        e_hi[:, kc, :], start=False,
                                        stop=(kc == 15))
                    return last
                ptg = grp("tensor", "sT", avmm)
                wt("vector", "sT", ptg)
                if ds == 0:
                    wt("vector", "sV", rbc_v)
                vtg = grp("vector", "sV", lambda v, ds=ds, bk_i=bk_i:
                          v.tensor_tensor(outf_ev[:, ds, :], pb[bk_i][:, 0:TQ],
                                          row_bc[:, 0:TQ], ALU.mult))
                bank_free[bk_i] = ("sV", vtg)
            wt("sync", "sV")
            dma(d_outf[:, seq * S + t0: seq * S + t0 + TQ]
                .rearrange("(c p) t -> p c t", p=128), outf_ev[...])

    w1_meanst = rk(0, [128, 2048], F32)

    def phase_w1(w_ready):
        h1f = w1_h1f
        h1hi_o, h1lo_o = qn_hi, qn_lo
        items = mean_items()
        grp("vector", "sV", lambda v: v.memset(wsum[...], 0.0))
        store_tgt = 0
        for it in range(T // TW):
            for _ in range(4):
                if items:
                    emit_mean_chunk(items.pop(0), w1_meanst)
            if it == T // TW - 1:
                assert not items, (len(items), 'mean chunks left')
            t0 = it * TW
            wt("sync", "sV")
            tg = dma(qf[...],
                     d_outf[:, t0:t0 + TW].rearrange("(c p) t -> p c t", p=128))
            wt("vector", "sD", tg)
            wt("vector", "sV", w_ready)
            vsp = grp("vector", "sV",
                      lambda v: split(v, qf[...], xt_hi[...], xt_lo[...]))
            for c in range(DC):
                bk_i = c % 2
                pe_wait_bank(bk_i)
                if c == 0:
                    wt("tensor", "sV", vsp)

                def mms(t, c=c, bk_i=bk_i):
                    for k in range(DC):
                        last = mm3(t, pb[bk_i][:, 0:TW],
                                   Wc_hi[:, k, c * 128:(c + 1) * 128],
                                   Wc_lo[:, k, c * 128:(c + 1) * 128],
                                   xt_hi[:, k, :], xt_lo[:, k, :],
                                   k == 0, k == DC - 1)
                    return last
                ptg = grp("tensor", "sT", mms)
                wt("scalar", "sT", ptg)
                wt("scalar", "sV")
                r = c % 2
                stg = grp("scalar", "sS", lambda sE, c=c, bk_i=bk_i, r=r:
                          sE.activation(h1f[:, r, :], pb[bk_i][:, 0:TW],
                                        AF.Relu, bias=b1_sb[:, c:c + 1]))
                wt("vector", "sS", stg)
                if c == 0:
                    wt("vector", "sD", store_tgt)
                vtg = grp("vector", "sV", lambda v, c=c, r=r:
                          split(v, h1f[:, r, :], h1hi_o[:, c, :], h1lo_o[:, c, :]))
                bank_free[bk_i] = ("sV", vtg)
            wt("sync", "sV")
            dma(d_h1h[:, t0:t0 + TW].rearrange("(c p) t -> p c t", p=128),
                h1hi_o[...])
            store_tgt = dma(d_h1l[:, t0:t0 + TW].rearrange("(c p) t -> p c t", p=128),
                            h1lo_o[...])

    def mean_items():
        items = []
        for d_wt, j in [(d_wgT, 0), (d_wuT, 1), (d_wdT, 2)]:
            ncol = d_wt.shape[1]
            nh = max(1, ncol // 2048)
            wc = ncol // nh
            for rchunk in range(d_wt.shape[0] // 128):
                for h in range(nh):
                    items.append((d_wt[rchunk * 128:(rchunk + 1) * 128,
                                       h * wc:(h + 1) * wc], j, wc))
        return items

    def emit_mean_chunk(item, stage):
        d_ap, j, wc = item
        wt("sync", "sV")
        tg = dma(stage[:, 0:wc], d_ap)
        wt("vector", "sD", tg)
        vchain(
            lambda v, wc=wc, stage=stage: v.reduce_sum(
                col_t0[...], stage[:, 0:wc], axis=AX.X,
                apply_absolute_value=True),
            lambda v, j=j: v.tensor_tensor(
                wsum[:, j:j + 1], wsum[:, j:j + 1], col_t0[...], ALU.add))

    def weight_mean_finalize():
        pe_wait_bank(PB_ROWS)
        wt("tensor", "sV")
        ptg = grp("tensor", "sT", lambda t:
                  t.matmul(pb[PB_ROWS][0:1, 0:3], ones_col32[...], wsum[...],
                           start=True, stop=True))
        wt("vector", "sT", ptg)
        vtg = vchain(
            lambda v: v.tensor_scalar_mul(row_a[0:1, 0:3], pb[PB_ROWS][0:1, 0:3],
                                          1.0 / (D * HF)),
            lambda v: v.tensor_scalar_max(row_a[0:1, 0:3], row_a[0:1, 0:3], EPS_Q),
            lambda v: v.tensor_scalar_mul(row_a[0:1, 3:6], row_a[0:1, 0:3], 0.5))
        bank_free[PB_ROWS] = ("sV", vtg)
        pe_wait_bank(PB_BC)
        wt("tensor", "sV", vtg)
        ptg = grp("tensor", "sT", lambda t:
                  t.matmul(pb[PB_BC][:, 0:6], ones_row32[...], row_a[0:1, 0:6],
                           start=True, stop=True))
        wt("vector", "sT", ptg)
        vtg = vchain(
            lambda v: (
                v.tensor_copy(md_g[...], pb[PB_BC][:, 0:1]),
                v.tensor_copy(md_u[...], pb[PB_BC][:, 1:2]),
                v.tensor_copy(md_d[...], pb[PB_BC][:, 2:3]),
                v.tensor_copy(thr_g[...], pb[PB_BC][:, 3:4]),
                v.tensor_copy(thr_u[...], pb[PB_BC][:, 4:5]),
                v.tensor_copy(thr_d[...], pb[PB_BC][:, 5:6]))[-1],
            lambda v: (
                v.tensor_scalar_mul(thr_gn[...], thr_g[...], -1.0),
                v.tensor_scalar_mul(thr_un[...], thr_u[...], -1.0),
                v.tensor_scalar_mul(thr_dn[...], thr_d[...], -1.0))[-1])
        bank_free[PB_BC] = ("sV", vtg)

    def round_chunk_emitters():
        items = []
        st_state = {"i": 0, "done": {}}

        def one(q):
            def emit():
                i = st_state["i"]
                b = i % 2
                if i >= 2:
                    wt("sync", "sV", st_state["done"][i - 2])
                tg = dma(wd_st[:, b, :], d_wdT[q * 128:(q + 1) * 128, :])
                wt("vector", "sD", tg)
                st_state["done"][i] = grp("vector", "sV", lambda v, q=q, b=b: (
                    v.tensor_scalar(wd_lt[...], wd_st[:, b, :], thr_dn[...],
                                    None, ALU.is_lt),
                    v.scalar_tensor_tensor(wdq8[:, q, :], wd_st[:, b, :],
                                           thr_d[...], wd_lt[...],
                                           ALU.is_gt, ALU.subtract))[-1])
                st_state["i"] = i + 1
            return emit
        for q in range(HC):
            items.append(one(q))
        return items

    def phase_w23(seq, round_items):
        a_banks = [2, 3, 4, 5]
        h1_tgt = {}
        build_bias_bcast(d_b3, rowd_w, b3bc)
        NC_ = 8
        CH = 512

        wst8a = w23_wst[:, :, 0:128]
        wst4 = rk(20480, [128, 4, 256], F32)

        def stage_weights(c):
            par = c % 2
            W2h, W2l, W3h, W3l = w23_views(par)
            ems = []
            for qq in range(4):
                c0 = qq * 128
                def em2(c=c, c0=c0, W2h=W2h, W2l=W2l):
                    wt("sync", "sV")
                    tg = dma(wst8a,
                             d_w2T[:, c * CH + c0: c * CH + c0 + 128]
                             .rearrange("(c p) t -> p c t", p=128))
                    wt("vector", "sD", tg)
                    grp("vector", "sV", lambda v: (
                        v.tensor_copy(W2h[:, :, c0:c0 + 128], wst8a),
                        v.tensor_tensor(W2l[:, :, c0:c0 + 128], wst8a,
                                        W2h[:, :, c0:c0 + 128],
                                        ALU.subtract))[-1])
                ems.append(em2)
            for qq in range(4):
                c0 = qq * 256
                def em3(c=c, c0=c0, W3h=W3h, W3l=W3l):
                    wt("sync", "sV")
                    tg = dma(wst4,
                             d_w3T[c * CH:(c + 1) * CH, c0:c0 + 256]
                             .rearrange("(c p) t -> p c t", p=128))
                    wt("vector", "sD", tg)
                    grp("vector", "sV", lambda v: (
                        v.tensor_copy(W3h[:, :, c0:c0 + 256], wst4),
                        v.tensor_tensor(W3l[:, :, c0:c0 + 256], wst4,
                                        W3h[:, :, c0:c0 + 256],
                                        ALU.subtract))[-1])
                ems.append(em3)
            return ems

        for em_ in stage_weights(0):
            em_()
        next_stage = []
        for c in range(NC_):
            par = c % 2
            W2h, W2l, W3h, W3l = w23_views(par)
            wrdy = cnt["sV"]
            if c + 1 < NC_:
                next_stage = stage_weights(c + 1)
            else:
                next_stage = []
            for it in range(S // TW):
                t0 = it * TW
                hb = it % 2
                if c == 0 and it == 0:
                    wt("sync", "sT")
                    dma(w23_h1h[:, 0],
                        d_h1h[:, seq * S: seq * S + TW]
                        .rearrange("(c p) t -> p c t", p=128))
                    h1_tgt[(0, 0)] = dma(
                        w23_h1l[:, 0],
                        d_h1l[:, seq * S: seq * S + TW]
                        .rearrange("(c p) t -> p c t", p=128))
                nxt = it + 1
                nc_, nit = (c, nxt) if nxt < S // TW else (c + 1, 0)
                if nc_ < NC_:
                    nt0 = nit * TW
                    wt("sync", "sT")
                    dma(w23_h1h[:, 1 - hb],
                        d_h1h[:, seq * S + nt0: seq * S + nt0 + TW]
                        .rearrange("(c p) t -> p c t", p=128))
                    h1_tgt[(nc_, nit)] = dma(
                        w23_h1l[:, 1 - hb],
                        d_h1l[:, seq * S + nt0: seq * S + nt0 + TW]
                        .rearrange("(c p) t -> p c t", p=128))
                wt("tensor", "sD", h1_tgt[(c, it)])
                wt("tensor", "sV", wrdy)
                for hs in range(4):
                    bk_i = hs % 2
                    pe_wait_bank(bk_i)

                    def h2mm(t, hs=hs, bk_i=bk_i, W2h=W2h, W2l=W2l, hb=hb):
                        for k in range(DC):
                            last = mm3(t, pb[bk_i][:, 0:TW],
                                       W2h[:, k, hs * 128:(hs + 1) * 128],
                                       W2l[:, k, hs * 128:(hs + 1) * 128],
                                       w23_h1h[:, hb, k, :], w23_h1l[:, hb, k, :],
                                       k == 0, k == DC - 1)
                        return last
                    ptg = grp("tensor", "sT", h2mm)
                    wt("scalar", "sT", ptg)
                    wt("scalar", "sV")
                    r = hs % 2
                    bidx = c * 4 + hs
                    stg = grp("scalar", "sS", lambda sE, bidx=bidx, bk_i=bk_i, r=r:
                              sE.activation(w23_h2f[:, r, :], pb[bk_i][:, 0:TW],
                                            AF.Relu,
                                            bias=b2_sb[:, bidx:bidx + 1]))
                    bank_free[bk_i] = ("sS", stg)
                    wt("vector", "sS", stg)
                    vtg = grp("vector", "sV", lambda v, r=r:
                              split(v, w23_h2f[:, r, :], w23_h2h[:, r, :],
                                    w23_h2l[:, r, :]))
                    if hs == 0:
                        for ab in a_banks:
                            pe_wait_bank(ab)
                    wt("tensor", "sV", vtg)

                    def w3mm(t, hs=hs, r=r, W3h=W3h, W3l=W3l):
                        for ts_ in range(2):
                            for dh in range(2):
                                ps = pb[a_banks[ts_ * 2 + dh]][:, 0:512]
                                last = mm3(t, ps,
                                           w23_h2h[:, r, ts_ * 128:(ts_ + 1) * 128],
                                           w23_h2l[:, r, ts_ * 128:(ts_ + 1) * 128],
                                           W3h[:, hs, dh * 512:(dh + 1) * 512],
                                           W3l[:, hs, dh * 512:(dh + 1) * 512],
                                           hs == 0, hs == 3)
                        return last
                    grp("tensor", "sT", w3mm)
                wt("vector", "sT")

                def a_evac(v, c=c, it=it):
                    for ts_ in range(2):
                        tc = it * 2 + ts_
                        for dh in range(2):
                            ps = pb[a_banks[ts_ * 2 + dh]][:, 0:512]
                            dst = a_acc[:, tc, dh * 512:(dh + 1) * 512]
                            if c == 0:
                                last = v.scalar_tensor_tensor(
                                    dst, ps, 1.0, b3bc[:, dh * 512:(dh + 1) * 512],
                                    ALU.mult, ALU.add)
                            else:
                                last = v.tensor_tensor(dst, dst, ps, ALU.add)
                    return last
                vtg = grp("vector", "sV", a_evac)
                for ab in a_banks:
                    bank_free[ab] = ("sV", vtg)
                if next_stage:
                    next_stage.pop(0)()
                if round_items and (c * 8 + it) % 4 == 1:
                    round_items.pop(0)()

    def phase_ln1(seq):
        store_guard = 0
        for tc in range(16):
            tile = seq * 16 + tc
            buf = tc % 2
            wt("sync", "sV")
            tg = dma(ln_x[:, buf, :], d_xtok[seq, tc * 128:(tc + 1) * 128, :])
            wt("vector", "sD", tg)
            vz = grp("vector", "sV", lambda v, tc=tc, buf=buf: (
                v.tensor_tensor(ln_z[...], ln_x[:, buf, :], a_acc[:, tc, :],
                                ALU.add),
                v.reduce_sum(col_m[...], ln_z[...], axis=AX.X))[-1])
            wt("scalar", "sV", vz)
            ssq = grp("scalar", "sS", lambda sE:
                      sE.activation(ln_zsq[...], ln_z[...], AF.Square))
            wt("vector", "sS", ssq)
            vchain(
                lambda v: v.reduce_sum(col_v[...], ln_zsq[...], axis=AX.X),
                lambda v: v.tensor_scalar_mul(col_m[...], col_m[...], 1.0 / D),
                lambda v: v.tensor_scalar_mul(col_v[...], col_v[...], 1.0 / D),
                lambda v: v.tensor_tensor(col_t0[...], col_m[...], col_m[...], ALU.mult),
                lambda v: v.tensor_tensor(col_v[...], col_v[...], col_t0[...],
                                          ALU.subtract))
            col_rsqrt_newton(col_v, col_r, col_t1, EPS_LN)
            if store_guard:
                wt("vector", "sD", store_guard)

            grp("vector", "sV", lambda v, buf=buf: (
                v.tensor_scalar(ln_x1[:, buf, :], ln_z[...], col_m[...], None,
                                ALU.subtract),
                v.tensor_scalar(ln_x1[:, buf, :], ln_x1[:, buf, :], col_r[...],
                                None, ALU.mult))[-1])
            vx1 = vchain(
                lambda v, buf=buf: v.reduce_max(col_am[...], ln_x1[:, buf, :],
                                                axis=AX.X,
                                                apply_absolute_value=True),
                lambda v: v.tensor_scalar_max(col_am[...], col_am[...], EPS_Q),
                lambda v: v.reciprocal(col_t0[...], col_am[...]),
                lambda v, tile=tile: (
                    v.tensor_scalar_mul(col_sx[:, tile:tile + 1], col_t0[...], 127.0),
                    v.tensor_tensor(col_t1[...], col_am[...], md_g[...], ALU.mult))[-1],
                lambda v, tile=tile: (
                    v.tensor_scalar_mul(col_dqg[:, tile:tile + 1], col_t1[...],
                                        1.0 / 127.0),
                    v.tensor_tensor(col_t1[...], col_am[...], md_u[...], ALU.mult))[-1],
                lambda v, tile=tile: v.tensor_scalar_mul(
                    col_rsu[:, tile:tile + 1], col_t1[...], 1.0 / 127.0))
            wt("sync", "sV", vx1)
            dma(d_x1[tile * 128:(tile + 1) * 128, :], ln_x1[:, buf, :])
            wt("scalar", "sV", vx1)
            sc = grp("scalar", "sS", lambda sE, buf=buf, tile=tile: (
                sE.activation(ln_c32[...], ln_x1[:, buf, :], AF.Copy,
                              scale=col_sx[:, tile:tile + 1]),
                sE.copy(ln_cf[...], ln_c32[...]))[-1])
            wt("vector", "sS", sc)

            def q_ops(v, buf=buf, tile=tile):
                v.scalar_tensor_tensor(ln_d[...], ln_x1[:, buf, :],
                                       col_sx[:, tile:tile + 1], ln_cf[...],
                                       ALU.mult, ALU.subtract)
                v.tensor_single_scalar(ln_gt[...], ln_d[...], 0.5, ALU.is_gt)
                v.tensor_single_scalar(ln_d[...], ln_d[...], -0.5, ALU.is_lt)
                v.tensor_tensor(ln_gt[...], ln_gt[...], ln_d[...], ALU.subtract)
                return v.tensor_tensor(ln_xq[:, buf, :], ln_cf[...], ln_gt[...],
                                       ALU.add)
            vq = grp("vector", "sV", q_ops)
            wt("sync", "sV", vq)
            store_guard = dma(d_xq[tile * 128:(tile + 1) * 128, :], ln_xq[:, buf, :])

    wround_state = {"i": 0, "done": {}}

    def weight_round(d_wt, dst_view, thr_p, thr_n):
        ncol = d_wt.shape[1]
        nh = max(1, ncol // 2048)
        w_cols = ncol // nh
        done_tgt = wround_state["done"]
        for q in range(d_wt.shape[0] // 128):
            for h in range(nh):
                i = wround_state["i"]
                b = i % 2
                if i >= 2:
                    wt("sync", "sV", done_tgt[i - 2])
                tg = dma(wr_st[:, b, 0:w_cols],
                         d_wt[q * 128:(q + 1) * 128, h * w_cols:(h + 1) * w_cols])
                wt("vector", "sD", tg)
                done_tgt[i] = grp("vector", "sV", lambda v, q=q, h=h, b=b: (
                    v.tensor_scalar(wr_lt[:, 0:w_cols], wr_st[:, b, 0:w_cols],
                                    thr_n[...], None, ALU.is_lt),
                    v.scalar_tensor_tensor(
                        dst_view[:, q, h * w_cols:(h + 1) * w_cols],
                        wr_st[:, b, 0:w_cols], thr_p[...], wr_lt[:, 0:w_cols],
                        ALU.is_gt, ALU.subtract))[-1])
                wround_state["i"] = i + 1
        return cnt["sV"]

    def phase_ffna(w_ready_V):
        pbT = [pb[i][...].bitcast(BF16) for i in range(8)]
        NT = T // 128
        load_tgt = {}
        tpose_done = {}
        pq_store = {}

        def load_xq(i):
            if i >= NT:
                return
            b = i % 2
            if i >= 2:
                wt("sync", "sT", tpose_done.get(i - 2, 0))
            load_tgt[i] = dma(fa_xq[:, b, :], d_xq[i * 128:(i + 1) * 128, :])

        def tpose_xq(i):
            if i >= NT:
                return
            b = i % 2
            wt("tensor", "sD", load_tgt[i])
            for k in range(DC):
                bk_i = 4 + k % 4
                pe_wait_bank(bk_i)
                ptg = grp("tensor", "sT", lambda t, k=k, b=b, bk_i=bk_i:
                          t.transpose(pbT[bk_i][:, 0:128],
                                      fa_xq[:, b, k * 128:(k + 1) * 128],
                                      id16_sb[...]))
                wt("scalar", "sT", ptg)
                stg = grp("scalar", "sS", lambda sE, k=k, b=b, bk_i=bk_i:
                          sE.copy(fa_xqT[:, b, k, :], pbT[bk_i][:, 0:128]))
                bank_free[bk_i] = ("sS", stg)
            tpose_done[i] = cnt["sT"]
            return cnt["sS"]

        load_xq(0)
        xqT_rdy = {0: tpose_xq(0)}
        load_xq(1)
        sil_guard = 0
        sp_ready = {}

        def quant_sc(qt, sl):
            rr = sl % 2
            cf = fa_cf[:, rr * 512:(rr + 1) * 512]
            if sl == 0:
                wt("scalar", "sV", sp_ready[qt])
            else:
                wt("scalar", "sV")
            return grp("scalar", "sS", lambda sE, sl=sl, cf=cf: (
                sE.activation(fa_c32[...], fa_p[:, sl * 512:(sl + 1) * 512],
                              AF.Copy, scale=col_t0[...]),
                sE.copy(cf, fa_c32[...]))[-1])

        def quant_v(qt, sl, sc):
            rr = sl % 2
            cf = fa_cf[:, rr * 512:(rr + 1) * 512]
            wt("vector", "sS", sc)
            if sl == 0:
                wt("vector", "sD", pq_store.get(qt - 1, 0))

            def qops(v, sl=sl, cf=cf):
                v.scalar_tensor_tensor(fa_d[...],
                                       fa_p[:, sl * 512:(sl + 1) * 512],
                                       col_t0[...], cf, ALU.mult,
                                       ALU.subtract)
                v.scalar_tensor_tensor(fa_gt[...], fa_d[...], 0.5,
                                       cf, ALU.is_gt, ALU.add)
                v.tensor_single_scalar(fa_d[...], fa_d[...], -0.5, ALU.is_lt)
                return v.tensor_tensor(fa_pq[:, sl * 512:(sl + 1) * 512],
                                       fa_gt[...], fa_d[...], ALU.subtract)
            g = grp("vector", "sV", qops)
            if sl == 7:
                wt("sync", "sV", g)
                pq_store[qt] = dma(d_pq[qt * 128:(qt + 1) * 128, :],
                                   fa_pq[...])
            return g

        for tile in range(NT):
            b = tile % 2
            wt("tensor", "sS", xqT_rdy[tile])
            if tile == 0:
                wt("tensor", "sV", w_ready_V)
            for hsl in range(8):
                bg, bu = (hsl % 2) * 2, (hsl % 2) * 2 + 1
                pe_wait_bank(bg)
                pe_wait_bank(bu)

                def gumm(t, b=b, hsl=hsl, bg=bg, bu=bu):
                    for k in range(DC):
                        t.matmul(pb[bg][:, 0:512], fa_xqT[:, b, k, :],
                                 wgq[:, k, hsl * 512:(hsl + 1) * 512],
                                 start=(k == 0), stop=(k == DC - 1))
                    for k in range(DC):
                        last = t.matmul(pb[bu][:, 0:512], fa_xqT[:, b, k, :],
                                        wuq[:, k, hsl * 512:(hsl + 1) * 512],
                                        start=(k == 0), stop=(k == DC - 1))
                    return last
                ptg = grp("tensor", "sT", gumm)
                if hsl == 3:
                    load_xq(tile + 2)
                    xqT_rdy[tile + 1] = tpose_xq(tile + 1)
                wt("scalar", "sT", ptg)
                if sil_guard:
                    wt("scalar", "sV", sil_guard)
                stg = grp("scalar", "sS", lambda sE, tile=tile, bg=bg:
                          sE.activation(fa_sil[...], pb[bg][:, 0:512], AF.Silu,
                                        scale=col_dqg[:, tile:tile + 1]))
                bank_free[bg] = ("sS", stg)
                wt("vector", "sS", stg)
                if tile > 0 and hsl == 0:
                    wt("vector", "sD", pq_store.get(tile - 1, 0))
                vtg = grp("vector", "sV", lambda v, hsl=hsl, bu=bu: (
                    v.tensor_tensor(fa_p[:, hsl * 512:(hsl + 1) * 512],
                                    pb[bu][:, 0:512], fa_sil[...], ALU.mult),
                    v.reduce_max(col_pp[:, hsl:hsl + 1],
                                 fa_p[:, hsl * 512:(hsl + 1) * 512], axis=AX.X,
                                 apply_absolute_value=True))[-1]
                          )
                sil_guard = vtg
                bank_free[bu] = ("sV", vtg)
                if tile > 0 and hsl < 7:
                    qsc = quant_sc(tile - 1, hsl + 1)
                    quant_v(tile - 1, hsl + 1, qsc)
            vchain(
                lambda v: v.reduce_max(col_am[...], col_pp[...], axis=AX.X),
                lambda v, tile=tile: v.tensor_tensor(
                    col_am[...], col_am[...], col_rsu[:, tile:tile + 1],
                    ALU.mult),
                lambda v: v.tensor_scalar_max(col_am[...], col_am[...], EPS_Q),
                lambda v: v.reciprocal(col_t0[...], col_am[...]),
                lambda v: (
                    v.tensor_scalar_mul(col_t0[...], col_t0[...], 127.0),
                    v.tensor_tensor(col_t1[...], col_am[...], md_d[...],
                                    ALU.mult))[-1],
                lambda v, tile=tile: (
                    v.tensor_scalar_mul(col_dqf[:, tile:tile + 1], col_t1[...],
                                        1.0 / 127.0)),
                lambda v, tile=tile: v.tensor_tensor(
                    col_t0[...], col_t0[...], col_rsu[:, tile:tile + 1],
                    ALU.mult))
            sp_ready[tile] = cnt["sV"]
            qsc = quant_sc(tile, 0)
            quant_v(tile, 0, qsc)
        for sl in range(1, 8):
            qsc = quant_sc(NT - 1, sl)
            quant_v(NT - 1, sl, qsc)

    def phase_ffnb(upc_ready):
        y_store = {}
        half_done = {}
        half_tg = {}
        tgx = {}

        def load_half(g, half):
            if g > 0:
                wt("sync", "sT", half_done[(g - 1, half)])
            tg = 0
            for c_ in range(HC):
                tg = dma(pqT_g[:, c_, half * 512:(half + 1) * 512],
                         d_pq[g * GB + half * 512: g * GB + (half + 1) * 512,
                              c_ * 128:(c_ + 1) * 128],
                         transpose=True)
            half_tg[(g, half)] = tg

        wt("sync", "sT")
        load_half(0, 0)
        wt("sync", "sV")
        for f in range(4):
            tgx[f] = dma(fb_x1[:, f % 4, :], d_x1[f * 128:(f + 1) * 128, :])
        load_half(0, 1)
        NG = T // GB
        for g in range(NG):
            for f in range(GB // 128):
                tile = g * (GB // 128) + f
                xbuf = f % 4
                buf = f % 2
                wt("tensor", "sD", half_tg[(g, 0 if f < 4 else 1)])
                if g == 0 and f == 0:
                    wt("tensor", "sV", upc_ready)
                for dh in range(2):
                    pe_wait_bank(dh)

                    def dmm(t, f=f, dh=dh):
                        for hc in range(HC):
                            last = t.matmul(pb[dh][:, 0:512],
                                            pqT_g[:, hc, f * 128:(f + 1) * 128],
                                            wdq[:, hc, dh * 512:(dh + 1) * 512],
                                            start=(hc == 0), stop=(hc == HC - 1))
                        return last
                    grp("tensor", "sT", dmm)
                ptg = cnt["sT"]
                if f == 3:
                    half_done[(g, 0)] = ptg
                elif f == 7:
                    half_done[(g, 1)] = ptg
                if f == 4 and g + 1 < NG:
                    load_half(g + 1, 0)
                elif f == 7 and g + 1 < NG:
                    load_half(g + 1, 1)
                wt("vector", "sT", ptg)
                wt("vector", "sD", tgx[tile])

                def z2ops(v, tile=tile, buf=xbuf):
                    for dh in range(2):
                        v.scalar_tensor_tensor(fb_z[:, dh * 512:(dh + 1) * 512],
                                               pb[dh][:, 0:512],
                                               col_dqf[:, tile:tile + 1],
                                               fb_x1[:, buf, dh * 512:(dh + 1) * 512],
                                               ALU.mult, ALU.add)
                    return v.reduce_sum(col_m[...], fb_z[...], axis=AX.X)
                vz = grp("vector", "sV", z2ops)
                bank_free[0] = ("sV", vz)
                bank_free[1] = ("sV", vz)
                if tile + 4 < T // 128:
                    wt("sync", "sV", vz)
                    tgx[tile + 4] = dma(fb_x1[:, xbuf, :],
                                        d_x1[(tile + 4) * 128:(tile + 5) * 128, :])
                wt("scalar", "sV", vz)
                ssq = grp("scalar", "sS", lambda sE:
                          sE.activation(fb_zsq[...], fb_z[...], AF.Square))
                wt("vector", "sS", ssq)
                vchain(
                    lambda v: v.reduce_sum(col_v[...], fb_zsq[...], axis=AX.X),
                    lambda v: v.tensor_scalar_mul(col_m[...], col_m[...], 1.0 / D),
                    lambda v: v.tensor_scalar_mul(col_v[...], col_v[...], 1.0 / D),
                    lambda v: v.tensor_tensor(col_t0[...], col_m[...], col_m[...],
                                              ALU.mult),
                    lambda v: v.tensor_tensor(col_v[...], col_v[...], col_t0[...],
                                              ALU.subtract))
                col_rsqrt_newton(col_v, col_r, col_t1, EPS_LN)
                if tile >= 2:
                    wt("vector", "sD", y_store.get(tile - 2, 0))
                vy = grp("vector", "sV", lambda v, buf=buf: (
                    v.tensor_scalar(fb_y[:, buf, :], fb_z[...], col_m[...], None,
                                    ALU.subtract),
                    v.tensor_scalar(fb_y[:, buf, :], fb_y[:, buf, :], col_r[...],
                                    None, ALU.mult))[-1])
                wt("sync", "sV", vy)
                y_store[tile] = dma(d_y[tile * 128:(tile + 1) * 128, :],
                                    fb_y[:, buf, :])

    for seq in range(NSEQ):
        sync_barrier()
        wr = stream_weight_Wc(d_w["wkT"], Wc_hi, Wc_lo)
        phase_kbuild(seq, wr)
        sync_barrier()
        wr = stream_weight_Wc(d_w["wvT"], Wc_hi, Wc_lo)
        phase_vbuild(seq, wr)
        sync_barrier()
        wr = stream_weight_Wc(d_w["wqT"], Wc_hi, Wc_lo)
        phase_attn(seq, wr)
    sync_barrier()
    wr = stream_weight_Wc(d_w["w1T"], Wc_hi, Wc_lo)
    phase_w1(wr)
    sync_barrier()
    weight_mean_finalize()
    round_items = round_chunk_emitters()
    for seq in range(NSEQ):
        sync_barrier()
        phase_w23(seq, round_items)
        sync_barrier()
        phase_ln1(seq)
    sync_barrier()
    for em_ in round_items:
        em_()
    weight_round(d_wgT, wgq, thr_g, thr_gn)
    weight_round(d_wuT, wuq, thr_u, thr_un)
    gu_ready = cnt["sV"]
    sync_barrier()
    phase_ffna(gu_ready)
    sync_barrier()
    wt("scalar", "sT")
    wt("scalar", "sV")
    grp("scalar", "sS", lambda sE:
        sE.copy(RA[...], RCI8[:, 0:32768]))
    upc_s = cnt["sS"]
    wt("vector", "sS", upc_s)
    upc_v = grp("vector", "sV", lambda v: v.tensor_copy(col_t0[...], col_t0[...]))
    phase_ffnb(upc_v)
    em("sync", lambda e, h=sem["sD"], v=cnt["sD"]: e.wait_ge(h, v))

    with nc.Block() as block:
        @block.tensor
        def _(t):
            for fn in plan["tensor"]:
                fn(t)

        @block.vector
        def _(v):
            for fn in plan["vector"]:
                fn(v)

        @block.scalar
        def _(sE):
            for fn in plan["scalar"]:
                fn(sE)

        @block.gpsimd
        def _(g_):
            for fn in plan["gpsimd"]:
                fn(g_)

        @block.sync
        def _(sy):
            for fn in plan["sync"]:
                fn(sy)
    ctx.close()
    return nc


_NC_CACHE = []


def kernel(L0_attn_x, L0_attn_Wq, L0_attn_bq, L0_attn_Wk, L0_attn_bk,
           L0_attn_Wv, L0_attn_bv, L0_attn_W1, L0_attn_b1, L0_attn_W2,
           L0_attn_b2, L0_attn_W3, L0_attn_b3, L0_ffn_w_gate, L0_ffn_w_up,
           L0_ffn_w_down):
    f32 = lambda a: np.ascontiguousarray(np.asarray(a, dtype=np.float32))
    x = f32(L0_attn_x)
    wT = {
        "wqT": f32(L0_attn_Wq).T, "wkT": f32(L0_attn_Wk).T,
        "wvT": f32(L0_attn_Wv).T, "w1T": f32(L0_attn_W1).T,
        "w2T": f32(L0_attn_W2).T, "w3T": f32(L0_attn_W3).T,
        "wgT": f32(L0_ffn_w_gate).T, "wuT": f32(L0_ffn_w_up).T,
        "wdT": f32(L0_ffn_w_down).T,
    }
    wT = {k: np.ascontiguousarray(v) for k, v in wT.items()}
    bcol = lambda b, nchunk: np.ascontiguousarray(
        f32(b).reshape(nchunk, 128).T)
    shared = {
        **wT,
        "bq": bcol(L0_attn_bq, DC), "bk": bcol(L0_attn_bk, DC),
        "b1": bcol(L0_attn_b1, DC), "b2": bcol(L0_attn_b2, HC),
        "bvrow": f32(L0_attn_bv).reshape(1, D),
        "b3row": f32(L0_attn_b3).reshape(1, D),
        "ident16": np.eye(128, dtype=ml_dtypes.bfloat16),
    }
    if not _NC_CACHE:
        _NC_CACHE.append(build())
    nc = _NC_CACHE[0]
    in_maps = []
    for c in range(NCORES):
        xs = x[NSEQ * c: NSEQ * (c + 1)]
        in_maps.append({
            "xT": np.ascontiguousarray(xs.transpose(0, 2, 1)),
            "xtok": np.ascontiguousarray(xs),
            **shared,
        })
    res = run_bass_kernel_spmd(nc, in_maps, core_ids=list(range(NCORES)))
    out = np.concatenate([np.asarray(r["y"]).reshape(NSEQ, S, D)
                          for r in res.results], axis=0)
    return out
